# version 22
# baseline (speedup 1.0000x reference)
import numpy as np
import ml_dtypes
import concourse.bass as bass
import concourse.mybir as mybir
from concourse.bass_utils import run_bass_kernel_spmd

F32 = mybir.dt.float32
BF16 = mybir.dt.bfloat16
U8 = mybir.dt.uint8
AF = mybir.ActivationFunctionType
ALU = mybir.AluOpType
AX = mybir.AxisListType

N_CORES = 8
SEQ = 2048
D = 1024
H = 16
TPC = 4096
NT = TPC // 128
NTS = SEQ // 128
MAGIC = 12582912.0
EXP_C = -40.0
EPS = 1e-5


class Sem:
    def __init__(self, nc, name):
        self.h = nc.alloc_semaphore(name=name)
        self.n = 0

    def bump(self, k=1):
        self.n += k
        return (self, k)


def build():
    nc = bass.Bass("TRN2", target_bir_lowering=False)

    x_d = nc.dram_tensor("x", [TPC, D], F32, kind="ExternalInput")
    wq_d = nc.dram_tensor("wq", [D, D], F32, kind="ExternalInput")
    wkv_d = nc.dram_tensor("wkv", [D, 2 * D], F32, kind="ExternalInput")
    wo_d = nc.dram_tensor("wo", [D, D], F32, kind="ExternalInput")
    wf_d = nc.dram_tensor("wf", [D, D], F32, kind="ExternalInput")
    gam_d = nc.dram_tensor("gam", [1, D], F32, kind="ExternalInput")
    bet_d = nc.dram_tensor("bet", [1, D], F32, kind="ExternalInput")
    mask_d = nc.dram_tensor("mask01", [128, 128], BF16, kind="ExternalInput")
    idf_d = nc.dram_tensor("idf", [128, 128], F32, kind="ExternalInput")
    idb_d = nc.dram_tensor("idb", [128, 128], BF16, kind="ExternalInput")
    onesr_d = nc.dram_tensor("onesr", [1, 128], F32, kind="ExternalInput")
    y_d = nc.dram_tensor("y", [TPC, D], F32, kind="ExternalOutput")
    x2s_d = nc.dram_tensor("x2s", [TPC, D], F32, kind="Internal")
    xn2s_d = nc.dram_tensor("xn2s", [TPC, D], F32, kind="Internal")
    cc_in_d = nc.dram_tensor("cc_in", [1, 16], F32, kind="Internal")
    cc_out_d = nc.dram_tensor("cc_out", [1, 16], F32, kind="Internal",
                              addr_space="Shared")

    A = nc.alloc_sbuf_tensor
    mask_sb = A("mask_sb", [128, 128], BF16)
    idf_sb = A("idf_sb", [128, 128], F32)
    idb_sb = A("idb_sb", [128, 128], BF16)
    onesr_sb = A("onesr_sb", [1, 128], F32)
    gam_sb = A("gam_sb", [1, D], F32)
    bet_sb = A("bet_sb", [1, D], F32)
    gb_sb = A("gb_sb", [128, D], F32)
    bb_sb = A("bb_sb", [128, D], F32)
    wqt_sb = A("wqt_sb", [128, 8 * D], BF16)
    bn12 = A("bn12", [128, 12], F32)
    bn2 = A("bn2", [128, 2], F32)
    vp = A("vp", [128, 1], F32)
    iv = A("iv", [128, 1], F32)
    rr = A("rr", [128, 1], F32)
    bcol = A("bcol", [128, 1], F32)
    bn12b = A("bn12b", [128, 12], F32)
    bn2b = A("bn2b", [128, 2], F32)
    vpb = A("vpb", [128, 1], F32)
    ivb = A("ivb", [128, 1], F32)
    rrb = A("rrb", [128, 1], F32)
    bcolb = A("bcolb", [128, 1], F32)
    mxcol = A("mxcol", [128, 1], F32)
    tmax = A("tmax", [128, 1], F32)
    tmax16 = A("tmax16", [128, 16], F32)
    SA = [dict(s6=A(f"sa6_{i}", [128, 12], F32), s2=A(f"sa2_{i}", [128, 2], F32),
               vp=A(f"savp_{i}", [128, 1], F32), iv=A(f"saiv_{i}", [128, 1], F32),
               rt=A(f"sar_{i}", [128, 1], F32), bt=A(f"sab_{i}", [128, 1], F32))
          for i in range(2)]
    SB = [dict(s6=A(f"sb6_{i}", [128, 12], F32), s2=A(f"sb2_{i}", [128, 2], F32),
               vp=A(f"sbvp_{i}", [128, 1], F32), iv=A(f"sbiv_{i}", [128, 1], F32),
               rt=A(f"sbr_{i}", [128, 1], F32), bt=A(f"sbb_{i}", [128, 1], F32))
          for i in range(2)]
    rc_sb = A("rc_sb", [128, 7], F32)
    ones_col = A("ones_col", [128, 1], F32)
    ws8 = A("ws8", [128, 8], F32)
    wm8 = A("wm8", [128, 8], F32)
    wc8 = A("wc8", [128, 8], F32)
    dcol = A("dcol", [128, 1], F32)
    ndcol = A("ndcol", [128, 1], F32)
    srow = A("srow", [1, 12], F32)
    gmr = A("gmr", [1, 4], F32)
    alpha = A("alpha", [1, 1], F32)
    invs_col = A("invs_col", [128, 1], F32)
    sa_col = A("sa_col", [128, 1], F32)

    AR, off = {}, 0

    def region(name, size):
        nonlocal off
        AR[name] = off
        off += (size + 31) & ~31

    region("xT", 8192)
    region("xn", 32768)
    region("wv0", 16384)
    region("wv1", 16384)
    region("K", 32768)
    region("V", 33280)
    region("Q", 32768)
    ARENA = off
    arena = A("arena", [128, ARENA], U8)
    abase = nc.lookup_mloc(arena).addr

    def place(name, shape, dtype, reg, extra=0):
        hnd = nc.virtual_sbuf_tensor(name, list(shape), dtype)
        nc._tensor(hnd.name, list(hnd.shape), hnd.dtype, type="SB")
        mloc = nc.lookup_mloc(hnd)
        mloc.allocated = True
        mloc.addr = abase + AR[reg] + extra
        return hnd

    xT = [place(f"xT{i}", [128, 1024], F32, "xT", 4096 * i) for i in range(2)]
    xn_fm = place("xn_fm", [128, 16384], BF16, "xn")
    attn_tm = place("attn_tm", [128, 16384], BF16, "xn")
    wv0 = place("wv0", [128, 8192], BF16, "wv0")
    wo_sb = place("wo_sb", [128, 8192], BF16, "wv0")
    wv1 = place("wv1", [128, 8192], BF16, "wv1")
    E_sb = place("E_sb", [128, 6144], BF16, "wv1")
    ternB = place("ternB", [128, 1024], BF16, "wv1", 12288)
    ternB2 = place("ternB2", [128, 1024], BF16, "K", 28672)
    yO2 = [place("yO", [128, 1024], F32, "wv1", 0),
           place("yO1", [128, 1024], F32, "Q", 0)]
    hO2 = [place("hO", [128, 1024], F32, "wv1", 4096),
           place("hO1", [128, 1024], F32, "Q", 4096)]
    K_fm = place("K_fm", [128, 16384], BF16, "K")
    x2O = [place(f"x2O{i}", [128, 1024], F32, "K", 4096 * i) for i in range(2)]
    wfb = [place(f"wfb{i}", [128, 1024], F32, "K", 8192 + 4096 * i) for i in range(2)]
    g1W = place("g1W", [128, 1024], F32, "K", 16384)
    g2W = place("g2W", [128, 1024], F32, "K", 20480)
    ternW = place("ternW", [128, 1024], F32, "K", 24576)
    t1O2 = [place("t1O", [128, 1024], F32, "K", 28672),
            place("t1O1", [128, 1024], F32, "Q", 8192)]
    V_aug = place("V_aug", [128, 16640], BF16, "V")
    attn_fm = place("attn_fm", [128, 16384], BF16, "V")
    Q_fm = place("Q_fm", [128, 16384], BF16, "Q")
    xnC = [place(f"xnC{i}", [128, 1024], F32, "Q", 4096 * i) for i in range(2)]
    t1C = place("t1C", [128, 1024], F32, "Q", 8192)
    intsT = [place(f"intsT{i}", [128, 1024], BF16, "Q", 12288 + 2048 * i) for i in range(2)]
    intsF = [place(f"intsF{i}", [128, 1024], BF16, "Q", 16384 + 2048 * i) for i in range(2)]
    outC = [place(f"outC{i}", [128, 1024], F32, "Q", 20480 + 4096 * i) for i in range(2)]

    psS = [nc.alloc_psum_tensor(f"psS{i}", [128, 1024], F32) for i in range(2)]
    psPV = [nc.alloc_psum_tensor(f"psPV{i}", [128, 512], F32) for i in range(3)]
    psX = nc.alloc_psum_tensor("psX", [128, 512], F32)

    s_in = Sem(nc, "s_in")
    s_w = Sem(nc, "s_w")
    s_tr = Sem(nc, "s_tr")
    s_mm = Sem(nc, "s_mm")
    s_v = Sem(nc, "s_v")
    s_a = Sem(nc, "s_a")
    s_st = Sem(nc, "s_st")
    s_cc = Sem(nc, "s_cc")
    s_g = Sem(nc, "s_g")
    s_y = Sem(nc, "s_y")

    sched = {k: [] for k in ("sync", "tensor", "vector", "scalar", "gpsimd")}

    def emit(eng, fn, waits=(), incs=()):
        waits = tuple((s.h, int(v)) for (s, v) in waits if v > 0)
        incs = tuple((s.h, int(k)) for (s, k) in incs)

        def run(e, fn=fn, waits=waits, incs=incs):
            for h, v in waits:
                e.wait_ge(h, v)
            inst = fn(e)
            for h, k in incs:
                inst.then_inc(h, k)
        sched[eng].append(run)

    def w(sem):
        return (sem, sem.n)

    def vwait(sem):
        return [(sem, sem.n)]

    tp_state = {"g": 0, "hist": []}

    def transpose_blocks(src_fn, dst_fn, n, first_waits, evac="vector"):
        first = True
        for g0 in range(0, n, 4):
            cnt = min(4, n - g0)
            buf = psX if tp_state["g"] % 2 == 0 else psPV[0]
            tp_state["g"] += 1
            guard = tp_state["hist"][-2] if len(tp_state["hist"]) >= 2 else None
            for j in range(g0, g0 + cnt):
                wl = []
                if j == g0:
                    wl = [guard] if guard else []
                    if first:
                        wl += list(first_waits)
                        first = False
                emit("tensor", lambda e, j=j, g0=g0, buf=buf: e.matmul(
                    buf[:, 128 * (j - g0):128 * (j - g0) + 128],
                    src_fn(j), idb_sb[:, :], start=True, stop=True),
                    waits=wl, incs=([s_mm.bump()] if j == g0 + cnt - 1 else []))
            if evac == "vector":
                emit("vector", lambda e, g0=g0, cnt=cnt, buf=buf: e.tensor_copy(
                    dst_fn(g0, cnt),
                    buf.ap()[:, 0:128 * cnt].rearrange("p (b r) -> p b r", r=128)),
                    waits=vwait(s_mm), incs=[s_v.bump()])
                tp_state["hist"].append((s_v, s_v.n))
            else:
                emit("scalar", lambda e, g0=g0, cnt=cnt, buf=buf: e.activation(
                    dst_fn(g0, cnt),
                    buf.ap()[:, 0:128 * cnt].rearrange("p (b r) -> p b r", r=128),
                    AF.Identity, bias=np.float32(0.0), scale=np.float32(1.0)),
                    waits=vwait(s_mm), incs=[s_a.bump()])
                tp_state["hist"].append((s_a, s_a.n))
        return tp_state["hist"][-1]

    for dst, src in ((mask_sb, mask_d), (idf_sb, idf_d), (idb_sb, idb_d),
                     (onesr_sb, onesr_d), (gam_sb, gam_d), (bet_sb, bet_d)):
        emit("sync", lambda e, d=dst, s=src: e.dma_start(out=d[:, :], in_=s[:, :]),
             incs=[s_in.bump(16)])
    consts_in = s_in.n
    for t, val in ((ones_col, 1.0), (mxcol, 0.0)):
        emit("vector", lambda e, t=t, v=val: e.memset(t[:, :], v))
    emit("vector", lambda e: e.memset(
        V_aug.ap().rearrange("p (t r) -> p t r", t=16)[:, :, 64:1040:65], 1.0))
    emit("vector", lambda e: e.drain(), incs=[s_v.bump()])
    memset_v = s_v.n
    for which, (src, dst) in enumerate(((gam_sb, gb_sb), (bet_sb, bb_sb))):
        for hf in range(2):
            emit("tensor", lambda e, s=src, hf=hf: e.matmul(
                psX[:, 0:512], onesr_sb[:, :], s[0:1, 512 * hf:512 * hf + 512],
                start=True, stop=True),
                waits=[(s_in, consts_in)] + ([] if which == 0 and hf == 0 else vwait(s_v)),
                incs=[s_mm.bump()])
            emit("vector", lambda e, d=dst, hf=hf: e.tensor_copy(
                d[:, 512 * hf:512 * hf + 512], psX[:, 0:512]),
                waits=vwait(s_mm), incs=[s_v.bump()])
    gbeta_v = s_v.n

    wkv3 = wkv_d.ap().rearrange("(c p) i -> p c i", p=128)
    wq3 = wq_d.ap().rearrange("(c p) i -> p c i", p=128)
    wo3 = wo_d.ap().rearrange("(c p) i -> p c i", p=128)
    wf3 = wf_d.ap().rearrange("(c p) i -> p c i", p=128)

    def stats_part1(src_ap, S, extra_waits=(), eps_mul=None):
        s6, s2, vpt, ivt, rt = (S[k] for k in ("s6", "s2", "vp", "iv", "rt"))
        emit("vector", lambda e, a=s6, b=src_ap: e.bn_stats(a[:, 0:6], b[:, 0:512]),
             waits=list(extra_waits))
        emit("vector", lambda e, a=s6, b=src_ap: e.bn_stats(a[:, 6:12], b[:, 512:1024]))
        emit("vector", lambda e: e.drain())
        emit("vector", lambda e, a=s2, b=s6: e.bn_aggr(a[:, :], b[:, :]))
        emit("vector", lambda e: e.drain())
        if eps_mul is None:
            emit("vector", lambda e, a=vpt, b=s2: e.tensor_scalar(
                a[:, :], b[:, 1:2], EPS, None, ALU.add))
        else:
            m, cadd = eps_mul
            emit("vector", lambda e, a=vpt, b=s2, m=m, c=cadd: e.tensor_scalar(
                a[:, :], b[:, 1:2], m, c, ALU.mult, ALU.add))
        emit("vector", lambda e: e.drain())
        emit("vector", lambda e, a=ivt, b=vpt: e.reciprocal(a[:, :], b[:, :]),
             incs=[s_v.bump()])
        emit("scalar", lambda e, a=rt, b=ivt: e.activation(a[:, :], b[:, :], AF.Sqrt),
             waits=vwait(s_v), incs=[s_a.bump()])
        return s_a.n

    def stats_part2(S, sqrt_a):
        s2, rt, bt = S["s2"], S["rt"], S["bt"]
        emit("vector", lambda e, a=bt, b=s2, c=rt: e.scalar_tensor_tensor(
            a[:, :], b[:, 0:1], -1.0, c[:, :], ALU.mult, ALU.mult),
            waits=[(s_a, sqrt_a)])
        emit("vector", lambda e: e.drain(), incs=[s_v.bump()])

    def stats_block2(src_ap, S, extra_waits=()):
        sa = stats_part1(src_ap, S, extra_waits)
        stats_part2(S, sa)

    def stats_block(src_ap, s6, s2, vpt, ivt, rt, bt, extra_waits=()):
        stats_block2(src_ap, dict(s6=s6, s2=s2, vp=vpt, iv=ivt, rt=rt, bt=bt),
                     extra_waits)

    o_evac_last = [None]
    for sq in range(2):
        tbase = NTS * sq

        vec_now = s_v.n
        emit("sync", lambda e, t=tbase: e.dma_start(
            out=xT[0][:, :], in_=x_d[128 * t:128 * (t + 1), :]),
            waits=[(s_v, vec_now)], incs=[s_in.bump(16)])
        emit("gpsimd", lambda e: e.dma_start(
            out=wv0.ap().rearrange("p (c i) -> p c i", c=8),
            in_=wkv3[:, :, 0:1024]),
            waits=[(s_mm, s_mm.n)], incs=[s_w.bump(16)])
        wk_ready = s_w.n

        evac_hist = []
        base_v = s_v.n
        base_st = s_st.n
        base_a = s_a.n
        slot_ctr = [0]

        def emit_proj_group(dst, w_sb, w_ready_val, tb, kc, extra):
            slot = slot_ctr[0]
            slot_ctr[0] += 1
            p = psS[slot % 2][:, 512 * ((slot // 2) % 2):
                              512 * ((slot // 2) % 2) + 512]
            guard = evac_hist[-4] if len(evac_hist) >= 4 else base_v
            for c in range(8):
                wl = []
                if c == 0:
                    wl = [(s_w, w_ready_val), (s_v, guard)] + extra
                emit("tensor", lambda e, kc=kc, tb=tb, c=c, p=p, ws=w_sb: e.matmul(
                    p, ws[:, 1024 * c + 128 * kc:1024 * c + 128 * kc + 128],
                    xn_fm[:, 2048 * c + 512 * tb:2048 * c + 512 * tb + 512],
                    start=(c == 0), stop=(c == 7)),
                    waits=wl, incs=([s_mm.bump()] if c == 7 else []))
            emit("vector", lambda e, kc=kc, tb=tb, p=p, d=dst: e.tensor_copy(
                d[:, 2048 * kc + 512 * tb:2048 * kc + 512 * tb + 512], p),
                waits=vwait(s_mm)
                + ([(s_st, base_st), (s_a, base_a)]
                   if len(evac_hist) == 0 else []),
                incs=[s_v.bump()])
            evac_hist.append(s_v.n)

        kq = [(tb, kc) for tb in range(4) for kc in range(8)]
        kq_i = [0]
        tr_ev_tile = {}
        xn_act_hist = {}
        tr_hist = {}
        for tt in range(NTS):
            buf = tt % 2
            xt_wait = s_in.n
            if tt + 1 < NTS:
                g = xn_act_hist.get((tt + 1) % 2, 0)
                emit("sync", lambda e, t=tbase + tt + 1, b=(tt + 1) % 2: e.dma_start(
                    out=xT[b][:, :], in_=x_d[128 * t:128 * (t + 1), :]),
                    waits=[(s_a, g), (s_v, vec_now)], incs=[s_in.bump(16)])
            stats_block(xT[buf], bn12, bn2, vp, iv, rr, bcol,
                        extra_waits=[(s_in, xt_wait)]
                        + ([(s_v, gbeta_v)] if tt == 0 and sq == 0 else []))
            emit("scalar", lambda e, b=buf: e.activation(
                E_sb[:, 2048 * b:2048 * b + 1024], xT[b][:, :], AF.Identity,
                bias=bcol[:, :], scale=rr[:, :]),
                waits=vwait(s_v) + ([(s_mm, tr_hist[buf])] if buf in tr_hist else [])
                + ([(s_g, s_g.n)] if tt == 0 else []),
                incs=[s_a.bump()])
            xn_act_hist[buf] = s_a.n
            tr_ev_tile[tt] = transpose_blocks(
                lambda j, b=buf: E_sb[:, 2048 * b + 128 * j:2048 * b + 128 * j + 128],
                lambda g0, cnt, t=tt: xn_fm.ap().rearrange(
                    "p (c r) -> p c r", c=8)[:, g0:g0 + cnt,
                                             128 * t:128 * t + 128],
                8, [(s_a, xn_act_hist[buf])])
            tr_hist[buf] = s_mm.n
            ready = 8 * ((tt - 3) // 4 + 1) if tt >= 3 else 0
            budget = 3
            while budget > 0 and kq_i[0] < ready:
                tb, kc = kq[kq_i[0]]
                emit_proj_group(K_fm, wv0, wk_ready, tb, kc,
                                [tr_ev_tile[4 * tb + 3]] if kc == 0 else [])
                kq_i[0] += 1
                budget -= 1
        while kq_i[0] < len(kq):
            tb, kc = kq[kq_i[0]]
            emit_proj_group(K_fm, wv0, wk_ready, tb, kc,
                            [tr_ev_tile[4 * tb + 3]] if kc == 0 else [])
            kq_i[0] += 1
        xn_tr_v = s_v.n

        emit("gpsimd", lambda e: e.dma_start(
            out=wv1.ap().rearrange("p (c i) -> p c i", c=8),
            in_=wkv3[:, :, 1024:2048]),
            waits=[tr_ev_tile[NTS - 1]], incs=[s_w.bump(16)])
        wv_ready = s_w.n

        emit("gpsimd", lambda e: e.dma_start(
            out=wv0.ap().rearrange("p (c i) -> p c i", c=8), in_=wq3),
            waits=[(s_mm, s_mm.n)], incs=[s_w.bump(16)])
        wq_ready = s_w.n

        emit("vector", lambda e: e.memset(
            V_aug.ap().rearrange("p (t r) -> p t r", t=16)[:, :, 64:1040:65], 1.0),
            waits=[(s_mm, s_mm.n)] + ([(s_v, memset_v)] if sq == 0 else [])
            + ([o_evac_last[0]] if o_evac_last[0] else []))
        emit("vector", lambda e: e.drain(), incs=[s_v.bump()])
        ones_v = s_v.n
        for tt in range(NTS):
            p = psS[tt % 2]
            guard = evac_hist[-2] if len(evac_hist) >= 2 else base_v
            for c in range(8):
                for hf in range(2):
                    wl = []
                    if c == 0 and hf == 0:
                        wl = [(s_w, wv_ready), (s_v, guard)]
                    emit("tensor", lambda e, t=tt, c=c, hf=hf, p=p: e.matmul(
                        p[:, 512 * hf:512 * hf + 512],
                        xn_fm[:, 2048 * c + 128 * t:2048 * c + 128 * t + 128],
                        wv1[:, 1024 * c + 512 * hf:1024 * c + 512 * hf + 512],
                        start=(c == 0), stop=(c == 7)),
                        waits=wl,
                        incs=([s_mm.bump()] if c == 7 and hf == 1 else []))
            emit("vector", lambda e, t=tt, p=p: e.tensor_copy(
                V_aug[:, 1040 * t:1040 * t + 1040]
                     .rearrange("p (h r) -> p h r", r=65)[:, :, 0:64],
                p.ap().rearrange("p (h r) -> p h r", h=16)),
                waits=vwait(s_mm) + ([(s_v, ones_v)] if tt == 0 else []),
                incs=[s_v.bump()])
            evac_hist.append(s_v.n)

        for tb in range(4):
            for kc in range(8):
                emit_proj_group(Q_fm, wv0, wq_ready, tb, kc, [])
        q_done_v = s_v.n
        q_done_mm = s_mm.n

        emit("gpsimd", lambda e: e.dma_start(
            out=wo_sb.ap().rearrange("p (c i) -> p c i", c=8), in_=wo3),
            waits=[(s_mm, q_done_mm)], incs=[s_w.bump(16)])
        wo_ready = s_w.n

        steps = []
        for h in range(H):
            for pas in range(2):
                for jj in (range(8) if pas == 0 else range(16)):
                    steps.append((h, pas, jj))
        exp_cnt = []
        mm_step = []
        pv_ev = s_v.n
        head_started_banks = {}
        head_first_pv = {}

        def emit_S(k):
            h, pas, jj = steps[k]
            hc, po = h // 2, 64 * (h % 2)
            ps = psS[k % 2]
            qlo = max(128 * jj, 1024 * pas)
            qhi = 1024 * (pas + 1)
            q0 = qlo
            first = True
            while q0 < qhi:
                q1 = min(q0 + 512, qhi,
                         1024 * pas + ((q0 - 1024 * pas) // 512 + 1) * 512)
                wl = []
                if first:
                    if k >= 2:
                        wl.append((s_a, exp_cnt[k - 2]))
                    else:
                        wl.append((s_v, q_done_v))
                    first = False
                emit("tensor", lambda e, q0=q0, q1=q1, p=ps, hc=hc,
                     po=po, jj=jj, pas=pas: e.matmul(
                         p[:, q0 - 1024 * pas:q1 - 1024 * pas],
                         K_fm[po:po + 64,
                              2048 * hc + 128 * jj:2048 * hc + 128 * jj + 128],
                         Q_fm[po:po + 64, 2048 * hc + q0:2048 * hc + q1],
                         start=True, stop=True),
                     waits=wl,
                     incs=([s_mm.bump()] if q1 == qhi else []))
                q0 = q1
            mm_step.append(s_mm.n)

        msk_of = {}

        def emit_exp_mask(k):
            h, pas, jj = steps[k]
            ps = psS[k % 2]
            eoff = 2048 * (k % 3)
            qlo = max(128 * jj, 1024 * pas)
            qhi = 1024 * (pas + 1)
            emit("scalar", lambda e, qlo=qlo, qhi=qhi, p=ps, eoff=eoff,
                 pas=pas: e.activation(
                     E_sb[:, eoff + qlo - 1024 * pas:eoff + qhi - 1024 * pas],
                     p[:, qlo - 1024 * pas:qhi - 1024 * pas],
                     AF.Exp, bias=np.float32(EXP_C), scale=np.float32(0.125)),
                 waits=[(s_mm, mm_step[k])], incs=[s_a.bump()])
            exp_cnt.append(s_a.n)
            if 1024 * pas <= 128 * jj < 1024 * (pas + 1):
                emit("vector", lambda e, jj=jj, eoff=eoff, pas=pas: e.tensor_tensor(
                    E_sb[:, eoff + 128 * jj - 1024 * pas:
                         eoff + 128 * jj - 1024 * pas + 128],
                    E_sb[:, eoff + 128 * jj - 1024 * pas:
                         eoff + 128 * jj - 1024 * pas + 128],
                    mask_sb[:, :], ALU.mult),
                    waits=[(s_a, exp_cnt[k])], incs=[s_v.bump()])
                msk_of[k] = s_v.n

        def emit_PV(k):
            nonlocal pv_ev
            h, pas, jj = steps[k]
            eoff = 2048 * (k % 3)
            ilo = max(jj, 8 * pas)
            ihi = 8 * (pas + 1)
            order = [i for i in range(ilo, ihi) if i != jj]
            if ilo <= jj < ihi:
                order.append(jj)
            started = head_started_banks.setdefault(h, set())
            for n_i, i in enumerate(order):
                bnk, sl = i // 7, i % 7
                wl = []
                if n_i == 0:
                    wl.append((s_a, exp_cnt[k]))
                    if not head_first_pv.get(h):
                        wl.append((s_v, pv_ev))
                        head_first_pv[h] = True
                if k in msk_of and i == jj:
                    wl.append((s_v, msk_of[k]))
                last = (jj == i)
                fin = last and (i in (6, 13, 15))
                st_flag = bnk not in started
                if st_flag:
                    started.add(bnk)
                emit("tensor", lambda e, i=i, jj=jj, h=h, eoff=eoff,
                     bnk=bnk, sl=sl, pas=pas, last=last, st=st_flag: e.matmul(
                         psPV[bnk][:, 65 * sl:65 * sl + 65],
                         E_sb[:, eoff + 128 * i - 1024 * pas:
                              eoff + 128 * i - 1024 * pas + 128],
                         V_aug[:, 1040 * jj + 65 * h:1040 * jj + 65 * h + 65],
                         start=st, stop=last),
                     waits=wl,
                     incs=([s_mm.bump()] if fin else []))
            if pas == 1 and jj == 15:
                base_mm = s_mm.n - 3
                for bnk, ntile in ((0, 7), (1, 7), (2, 2)):
                    emit("vector", lambda e, bnk=bnk, nt=ntile: e.reciprocal(
                        rc_sb[:, 0:nt], psPV[bnk][:, 64:64 + (nt - 1) * 65 + 1:65]),
                        waits=[(s_mm, base_mm + bnk + 1)])
                    emit("vector", lambda e: e.drain())
                    emit("vector", lambda e, bnk=bnk, nt=ntile, h=h: e.tensor_tensor(
                        attn_tm.ap().rearrange("p (t r) -> p t r", r=1024)
                               [:, 7 * bnk:7 * bnk + nt, 64 * h:64 * h + 64],
                        psPV[bnk][:, 0:nt * 65].rearrange("p (t r) -> p t r", r=65)
                                [:, :, 0:64],
                        rc_sb[:, 0:nt].unsqueeze(2).broadcast_to([128, nt, 64]),
                        ALU.mult))
                    emit("vector", lambda e: e.drain(), incs=[s_v.bump()])
                pv_ev = s_v.n

        emit_S(0)
        for k in range(len(steps)):
            emit_exp_mask(k)
            if k + 1 < len(steps):
                emit_S(k + 1)
            if k >= 1:
                emit_PV(k - 1)
        emit_PV(len(steps) - 1)
        attn_v = s_v.n
        attn_mm = s_mm.n
        exp_last = exp_cnt[-1]

        vec_now = s_v.n
        emit("sync", lambda e, t=tbase: e.dma_start(
            out=xT[0][:, :], in_=x_d[128 * t:128 * (t + 1), :]),
            waits=[(s_a, xn_act_hist[0])], incs=[s_in.bump(16)])
        x2v_hist, apply_v, st_hist, gh_hist = {}, {}, {}, {}
        h_v, mx_v_of, pend = {}, {}, {}
        mx_v_last = [0]
        tr_v_of = {}

        def o_transposes(t2, first):
            tr_v_of[t2] = transpose_blocks(
                lambda j, t=t2: attn_tm[:, 1024 * t + 128 * j:1024 * t + 128 * j + 128],
                lambda g0, cnt, t=t2: attn_fm.ap().rearrange(
                    "p (c r) -> p c r", c=8)[:, g0:g0 + cnt,
                                             128 * t:128 * t + 128],
                8, [(s_v, attn_v)] if first else [], evac="scalar")

        def emit_redmax(pt):
            b = pt % 2
            emit("vector", lambda e, b=b, pt=pt: e.reduce_max(
                tmax16[:, pt:pt + 1], t1O2[b][:, :], axis=AX.X,
                apply_absolute_value=True),
                waits=[(s_g, gh_hist[(pt, "g")])], incs=[s_v.bump()])
            mx_v_of[b] = s_v.n
            mx_v_last[0] = s_v.n

        def apply_unit(pt):
            b = pt % 2
            Sa = SA[b]
            sa_a = pend[b]
            stats_part2(Sa, sa_a)
            emit("vector", lambda e, b=b, Sa=Sa: e.tensor_scalar(
                hO2[b][:, :], x2O[b][:, :], Sa["rt"][:, :], Sa["bt"][:, :],
                ALU.mult, ALU.add),
                waits=[(s_g, gh_hist.get(b, 0))], incs=[s_v.bump()])
            h_v[b] = apply_v[b] = s_v.n
            emit("gpsimd", lambda e, b=b: e.tensor_mul(
                t1O2[b][:, :], hO2[b][:, :], gb_sb[:, :]),
                waits=[(s_v, h_v[b]), (s_st, st_hist.get(("xn2", b), 0)),
                       (s_v, mx_v_of.get(b, 0))])
            emit("gpsimd", lambda e, b=b: e.tensor_add(
                t1O2[b][:, :], t1O2[b][:, :], bb_sb[:, :]), incs=[s_g.bump()])
            gh_hist[b] = gh_hist[(pt, "g")] = s_g.n
            emit("sync", lambda e, t=tbase + pt, b=b: e.dma_start(
                out=xn2s_d[128 * t:128 * (t + 1), :], in_=t1O2[b][:, :]),
                waits=[(s_g, gh_hist[b])], incs=[s_st.bump(16)])
            st_hist[("xn2", b)] = s_st.n
            if pt > 0:
                emit_redmax(pt - 1)

        o_transposes(0, True)
        for tt in range(NTS):
            buf = tt % 2
            xt_wait = s_in.n
            if tt + 1 < NTS:
                g = x2v_hist.get((tt + 1) % 2, 0)
                ga = xn_act_hist[(tt + 1) % 2] if tt == 0 else 0
                emit("sync", lambda e, t=tbase + tt + 1, b=(tt + 1) % 2,
                     g=g, ga=ga: e.dma_start(
                         out=xT[b][:, :], in_=x_d[128 * t:128 * (t + 1), :]),
                     waits=[(s_v, g), (s_a, ga)], incs=[s_in.bump(16)])
                o_transposes(tt + 1, False)
            p = psS[tt % 2]
            for c in range(8):
                for hf in range(2):
                    wl = []
                    if c == 0 and hf == 0:
                        wl = [tr_v_of[tt], (s_w, wo_ready)]
                        if tt < 2:
                            wl.append((s_a, exp_last))
                        else:
                            wl.append((s_v, x2v_hist[tt % 2]))
                    emit("tensor", lambda e, t=tt, c=c, hf=hf, p=p: e.matmul(
                        p[:, 512 * hf:512 * hf + 512],
                        attn_fm[:, 2048 * c + 128 * t:2048 * c + 128 * t + 128],
                        wo_sb[:, 1024 * c + 512 * hf:1024 * c + 512 * hf + 512],
                        start=(c == 0), stop=(c == 7)),
                        waits=wl,
                        incs=([s_mm.bump()] if c == 7 and hf == 1 else []))
            emit("vector", lambda e, b=buf, p=p: e.tensor_add(
                x2O[b][:, :], p[:, :], xT[b][:, :]),
                waits=vwait(s_mm) + [(s_in, xt_wait),
                                     (s_st, st_hist.get(buf, 0))])
            emit("vector", lambda e: e.drain(), incs=[s_v.bump()])
            x2v_hist[buf] = s_v.n
            emit("sync", lambda e, t=tbase + tt, b=buf: e.dma_start(
                out=x2s_d[128 * t:128 * (t + 1), :], in_=x2O[b][:, :]),
                waits=[(s_v, x2v_hist[buf])], incs=[s_st.bump(16)])
            st_hist[buf] = s_st.n
            pend[buf] = stats_part1(x2O[buf], SA[buf],
                                    eps_mul=(1.0 + EPS, EPS * EPS))
            if tt > 0:
                apply_unit(tt - 1)
        apply_unit(NTS - 1)
        emit_redmax(NTS - 1)
        emit("vector", lambda e: e.drain())
        emit("vector", lambda e: e.reduce_max(tmax[:, :], tmax16[:, :], axis=AX.X))
        emit("vector", lambda e: e.drain())
        emit("vector", lambda e: e.tensor_max(mxcol[:, :], mxcol[:, :], tmax[:, :]))
        emit("vector", lambda e: e.drain(), incs=[s_v.bump()])
        mx_v_last[0] = s_v.n
        o_evac_last[0] = tr_v_of[NTS - 1]

        if sq == 0:
            emit("gpsimd", lambda e: e.dma_start(out=wfb[0][:, :], in_=wf3[:, 0, :]),
                 waits=[(s_v, attn_v)], incs=[s_w.bump(16)])
            wf_rd = {}
            for c in range(8):
                wf_wait = s_w.n
                if c + 1 < 8:
                    emit("gpsimd", lambda e, c=c: e.dma_start(
                        out=wfb[(c + 1) % 2][:, :], in_=wf3[:, c + 1, :]),
                        waits=[(s_v, wf_rd.get((c + 1) % 2, 0))],
                        incs=[s_w.bump(16)])
                emit("vector", lambda e, c=c: e.reduce_sum(
                    ws8[:, c:c + 1], wfb[c % 2][:, :], axis=AX.X,
                    apply_absolute_value=True),
                    waits=[(s_w, wf_wait)], incs=[s_v.bump()])
                wf_rd[c % 2] = s_v.n
            emit("tensor", lambda e: e.matmul(
                psX[0:1, 0:8], ones_col[:, :], ws8[:, :], start=True, stop=True),
                waits=vwait(s_v), incs=[s_mm.bump()])
            emit("vector", lambda e: e.reduce_sum(srow[0:1, 0:1], psX[0:1, 0:8],
                                                  axis=AX.X),
                 waits=vwait(s_mm))
            emit("vector", lambda e: e.drain())
            emit("vector", lambda e: e.tensor_scalar(
                srow[0:1, 1:2], srow[0:1, 0:1], 0.7 / (1024.0 * 1024.0), None, ALU.mult))
            emit("vector", lambda e: e.drain(), incs=[s_v.bump()])
            emit("tensor", lambda e: e.matmul(
                psX[:, 128:129], onesr_sb[:, :], srow[0:1, 1:2], start=True, stop=True),
                waits=vwait(s_v), incs=[s_mm.bump()])
            emit("vector", lambda e: e.tensor_copy(dcol[:, :], psX[:, 128:129]),
                 waits=vwait(s_mm))
            emit("vector", lambda e: e.drain())
            emit("vector", lambda e: e.tensor_scalar(ndcol[:, :], dcol[:, :], -1.0,
                                                     None, ALU.mult))
            emit("vector", lambda e: e.drain(), incs=[s_v.bump()])
            dcol_v = s_v.n
            emit("gpsimd", lambda e: e.dma_start(out=wfb[0][:, :], in_=wf3[:, 0, :]),
                 waits=[(s_v, dcol_v)], incs=[s_w.bump(16)])
            tr_mm_of = {}
            tern_a_of = {}
            wf_rd = {}
            tB = [ternB, ternB2]
            st_xn2_0 = st_hist.get(("xn2", 0), 0)
            for c in range(8):
                wf_wait = s_w.n
                if c + 1 < 8:
                    emit("gpsimd", lambda e, c=c: e.dma_start(
                        out=wfb[(c + 1) % 2][:, :], in_=wf3[:, c + 1, :]),
                        waits=[(s_v, wf_rd.get((c + 1) % 2, 0))],
                        incs=[s_w.bump(16)])
                b = c % 2
                emit("vector", lambda e, b=b: e.tensor_scalar(
                    g1W[:, :], wfb[b][:, :], dcol[:, :], None, ALU.is_gt),
                    waits=[(s_w, wf_wait)])
                emit("vector", lambda e, b=b: e.tensor_scalar(
                    g2W[:, :], wfb[b][:, :], ndcol[:, :], None, ALU.is_lt))
                emit("vector", lambda e: e.drain())
                emit("vector", lambda e: e.tensor_sub(ternW[:, :], g1W[:, :], g2W[:, :]),
                     waits=([(s_a, tern_a_of[c - 1])] if c > 0 else []))
                emit("vector", lambda e: e.drain(), incs=[s_v.bump()])
                tern_w_v = s_v.n
                emit("vector", lambda e, c=c: e.reduce_sum(
                    wc8[:, c:c + 1], ternW[:, :], axis=AX.X,
                    apply_absolute_value=True))
                emit("vector", lambda e, b=b: e.tensor_mul(g1W[:, :], wfb[b][:, :],
                                                           ternW[:, :]))
                emit("vector", lambda e: e.drain())
                emit("vector", lambda e, c=c: e.reduce_sum(
                    wm8[:, c:c + 1], g1W[:, :], axis=AX.X), incs=[s_v.bump()])
                wf_rd[b] = s_v.n
                emit("scalar", lambda e, b=b: e.activation(
                    tB[b][:, :], ternW[:, :], AF.Identity,
                    bias=np.float32(0.0), scale=np.float32(1.0)),
                    waits=[(s_v, tern_w_v)]
                    + ([(s_mm, tr_mm_of[c - 2])] if c > 1 else [])
                    + ([(s_st, st_xn2_0)] if c == 1 else []),
                    incs=[s_a.bump()])
                tern_a_of[c] = s_a.n
                transpose_blocks(
                    lambda ic, b=b: tB[b][:, 128 * ic:128 * ic + 128],
                    lambda g0, cnt, c=c: wqt_sb.ap().rearrange(
                        "p (ic r) -> p ic r", ic=8)[:, g0:g0 + cnt,
                                                    128 * c:128 * c + 128],
                    8, [(s_a, tern_a_of[c])], evac="scalar")
                tr_mm_of[c] = s_mm.n
            emit("tensor", lambda e: e.matmul(
                psX[0:1, 129:137], ones_col[:, :], wm8[:, :], start=True, stop=True),
                waits=vwait(s_v), incs=[s_mm.bump()])
            emit("tensor", lambda e: e.matmul(
                psX[0:1, 137:145], ones_col[:, :], wc8[:, :], start=False, stop=True),
                incs=[s_mm.bump()])
            emit("vector", lambda e: e.reduce_sum(
                srow[0:1, 8:9], psX[0:1, 129:137], axis=AX.X),
                waits=vwait(s_mm))
            emit("vector", lambda e: e.reduce_sum(
                srow[0:1, 9:10], psX[0:1, 137:145], axis=AX.X))
            emit("vector", lambda e: e.drain())
            emit("vector", lambda e: e.tensor_scalar(
                srow[0:1, 2:3], srow[0:1, 9:10], 1.0, None, ALU.max))
            emit("vector", lambda e: e.drain())
            emit("vector", lambda e: e.reciprocal(srow[0:1, 3:4], srow[0:1, 2:3]))
            emit("vector", lambda e: e.drain())
            emit("vector", lambda e: e.tensor_mul(alpha[0:1, :], srow[0:1, 8:9],
                                                  srow[0:1, 3:4]))
            emit("vector", lambda e: e.drain(), incs=[s_v.bump()])

    emit("tensor", lambda e: e.matmul(
        psX[0:1, 0:128], mxcol[:, :], idf_sb[:, :], start=True, stop=True),
        waits=vwait(s_v), incs=[s_mm.bump()])
    emit("vector", lambda e: e.reduce_max(gmr[0:1, 0:1], psX[0:1, 0:128], axis=AX.X),
         waits=vwait(s_mm))
    emit("vector", lambda e: e.drain(), incs=[s_v.bump()])
    emit("sync", lambda e: e.dma_start(out=cc_in_d[0:1, 0:1], in_=gmr[0:1, 0:1]),
         waits=vwait(s_v), incs=[s_st.bump(16)])
    emit("gpsimd", lambda e: e.collective_compute(
        "AllReduce", ALU.max, replica_groups=[list(range(N_CORES))],
        ins=[cc_in_d.ap().opt()], outs=[cc_out_d.ap().opt()]),
        waits=[w(s_st)], incs=[s_cc.bump(1)])
    emit("sync", lambda e: e.dma_start(out=gmr[0:1, 1:2], in_=cc_out_d[0:1, 0:1]),
         waits=[w(s_cc)], incs=[s_in.bump(16)])
    emit("vector", lambda e: e.tensor_scalar(gmr[0:1, 2:3], gmr[0:1, 1:2], 1e-8,
                                             None, ALU.max),
         waits=[w(s_in)])
    emit("vector", lambda e: e.drain())
    emit("vector", lambda e: e.reciprocal(gmr[0:1, 3:4], gmr[0:1, 2:3]))
    emit("vector", lambda e: e.drain())
    emit("vector", lambda e: e.tensor_scalar(srow[0:1, 4:5], gmr[0:1, 3:4], 127.0,
                                             None, ALU.mult))
    emit("vector", lambda e: e.tensor_scalar(srow[0:1, 5:6], gmr[0:1, 2:3], 1.0 / 127.0,
                                             None, ALU.mult))
    emit("vector", lambda e: e.drain())
    emit("vector", lambda e: e.tensor_mul(srow[0:1, 6:7], srow[0:1, 5:6], alpha[0:1, :]))
    emit("vector", lambda e: e.drain(), incs=[s_v.bump()])
    emit("tensor", lambda e: e.matmul(
        psX[:, 200:201], onesr_sb[:, :], srow[0:1, 4:5], start=True, stop=True),
        waits=vwait(s_v), incs=[s_mm.bump()])
    emit("tensor", lambda e: e.matmul(
        psX[:, 201:202], onesr_sb[:, :], srow[0:1, 6:7], start=False, stop=True),
        incs=[s_mm.bump()])
    emit("vector", lambda e: e.tensor_copy(invs_col[:, :], psX[:, 200:201]),
         waits=vwait(s_mm))
    emit("vector", lambda e: e.tensor_copy(sa_col[:, :], psX[:, 201:202]))
    emit("vector", lambda e: e.drain(), incs=[s_v.bump()])
    scale_v = s_v.n
    x2w_done = s_st.n

    emit("sync", lambda e: e.dma_start(out=xnC[0][:, :], in_=xn2s_d[0:128, :]),
         waits=[(s_st, x2w_done), (s_mm, s_mm.n)], incs=[s_in.bump(16)])
    emit("sync", lambda e: e.dma_start(out=xT[0][:, :], in_=x2s_d[0:128, :]),
         incs=[s_in.bump(16)])
    ints_hist, out_hist = {}, {}
    trC_hist, mmC_hist = {}, {}
    for tt in range(NT):
        buf = tt % 2
        in_wait = s_in.n
        if tt + 1 < NT:
            b2 = (tt + 1) % 2
            emit("sync", lambda e, t=tt + 1, b=b2, g=ints_hist.get(b2, 0): e.dma_start(
                out=xnC[b][:, :], in_=xn2s_d[128 * t:128 * (t + 1), :]),
                waits=[(s_v, g)], incs=[s_in.bump(16)])
            emit("sync", lambda e, t=tt + 1, b=b2, g=out_hist.get(b2, 0): e.dma_start(
                out=xT[b][:, :], in_=x2s_d[128 * t:128 * (t + 1), :]),
                waits=[(s_v, g)], incs=[s_in.bump(16)])
        emit("vector", lambda e, b=buf: e.tensor_scalar(
            t1C[:, :], xnC[b][:, :], invs_col[:, :], MAGIC, ALU.mult, ALU.add),
            waits=[(s_in, in_wait), (s_v, scale_v)]
            + ([(s_mm, trC_hist[buf][1])] if buf in trC_hist else []))
        emit("vector", lambda e: e.drain())
        emit("vector", lambda e, b=buf: e.tensor_scalar(
            intsT[b][:, :], t1C[:, :], -MAGIC, None, ALU.add))
        emit("vector", lambda e: e.drain(), incs=[s_v.bump()])
        ints_hist[buf] = s_v.n
        trC_hist[buf] = (transpose_blocks(
            lambda j, b=buf: intsT[b][:, 128 * j:128 * j + 128],
            lambda g0, cnt, b=buf: intsF[b][:, 128 * g0:128 * (g0 + cnt)]
                .rearrange("p (b2 r) -> p b2 r", r=128),
            8, [(s_v, ints_hist[buf])], evac="scalar"), s_mm.n)
        p = psS[tt % 2]
        for ic in range(8):
            for hf in range(2):
                wl = []
                if ic == 0 and hf == 0:
                    wl = [trC_hist[buf][0], (s_v, out_hist.get(buf, 0))]
                emit("tensor", lambda e, b=buf, ic=ic, hf=hf, p=p: e.matmul(
                    p[:, 512 * hf:512 * hf + 512],
                    intsF[b][:, 128 * ic:128 * ic + 128],
                    wqt_sb[:, 1024 * ic + 512 * hf:1024 * ic + 512 * hf + 512],
                    start=(ic == 0), stop=(ic == 7)),
                    waits=wl,
                    incs=([s_mm.bump()] if ic == 7 and hf == 1 else []))
        mmC_hist[buf] = s_mm.n
        emit("vector", lambda e, b=buf, p=p: e.scalar_tensor_tensor(
            outC[b][:, :], p[:, :], sa_col[:, :], xT[b][:, :], ALU.mult, ALU.add),
            waits=vwait(s_mm))
        emit("vector", lambda e: e.drain(), incs=[s_v.bump()])
        out_hist[buf] = s_v.n
        emit("sync", lambda e, t=tt, b=buf: e.dma_start(
            out=y_d[128 * t:128 * (t + 1), :], in_=outC[b][:, :]),
            waits=[(s_v, out_hist[buf])], incs=[s_y.bump(16)])
    emit("sync", lambda e: e.nop(), waits=[w(s_y)])

    with nc.Block() as blk:
        blk.sync(lambda e: [f(e) for f in sched["sync"]])
        blk.tensor(lambda e: [f(e) for f in sched["tensor"]])
        blk.vector(lambda e: [f(e) for f in sched["vector"]])
        blk.scalar(lambda e: [f(e) for f in sched["scalar"]])
        blk.gpsimd(lambda e: [f(e) for f in sched["gpsimd"]])
    return nc


_NC_CACHE = None


def kernel(L0_attn_x, L0_attn_Wq, L0_attn_Wkv, L0_attn_Wo,
           L0_ffn_weight, L0_ffn_gamma, L0_ffn_beta):
    global _NC_CACHE
    x = np.ascontiguousarray(np.asarray(L0_attn_x, dtype=np.float32))
    wq = np.ascontiguousarray(np.asarray(L0_attn_Wq, dtype=np.float32))
    wkv = np.ascontiguousarray(np.asarray(L0_attn_Wkv, dtype=np.float32))
    wo = np.ascontiguousarray(np.asarray(L0_attn_Wo, dtype=np.float32))
    wf = np.ascontiguousarray(np.asarray(L0_ffn_weight, dtype=np.float32))
    gam = np.ascontiguousarray(np.asarray(L0_ffn_gamma, dtype=np.float32)).reshape(1, D)
    bet = np.ascontiguousarray(np.asarray(L0_ffn_beta, dtype=np.float32)).reshape(1, D)
    mask01 = np.triu(np.ones((128, 128), dtype=np.float32)).astype(ml_dtypes.bfloat16)
    idf = np.eye(128, dtype=np.float32)
    idb = np.eye(128, dtype=np.float32).astype(ml_dtypes.bfloat16)
    onesr = np.ones((1, 128), dtype=np.float32)

    if _NC_CACHE is None:
        _NC_CACHE = build()
    nc = _NC_CACHE
    in_maps = []
    for c in range(N_CORES):
        shard = np.ascontiguousarray(x[2 * c:2 * c + 2].reshape(TPC, D))
        in_maps.append(dict(x=shard, wq=wq, wkv=wkv, wo=wo, wf=wf,
                            gam=gam, bet=bet, mask01=mask01, idf=idf,
                            idb=idb, onesr=onesr))
    res = run_bass_kernel_spmd(nc, in_maps, core_ids=list(range(N_CORES)))
    out = np.concatenate(
        [np.asarray(res.results[c]["y"]).reshape(2, SEQ, D) for c in range(N_CORES)],
        axis=0)
    return out.astype(np.float32)


# version 23
# speedup vs baseline: 1.0084x; 1.0084x over previous
import numpy as np
import ml_dtypes
import concourse.bass as bass
import concourse.mybir as mybir
from concourse.bass_utils import run_bass_kernel_spmd

F32 = mybir.dt.float32
BF16 = mybir.dt.bfloat16
U8 = mybir.dt.uint8
AF = mybir.ActivationFunctionType
ALU = mybir.AluOpType
AX = mybir.AxisListType

N_CORES = 8
SEQ = 2048
D = 1024
H = 16
TPC = 4096
NT = TPC // 128
NTS = SEQ // 128
MAGIC = 12582912.0
EXP_C = -40.0
EPS = 1e-5


class Sem:
    def __init__(self, nc, name):
        self.h = nc.alloc_semaphore(name=name)
        self.n = 0

    def bump(self, k=1):
        self.n += k
        return (self, k)


def build():
    nc = bass.Bass("TRN2", target_bir_lowering=False)

    x_d = nc.dram_tensor("x", [TPC, D], F32, kind="ExternalInput")
    wq_d = nc.dram_tensor("wq", [D, D], F32, kind="ExternalInput")
    wkv_d = nc.dram_tensor("wkv", [D, 2 * D], F32, kind="ExternalInput")
    wo_d = nc.dram_tensor("wo", [D, D], F32, kind="ExternalInput")
    wf_d = nc.dram_tensor("wf", [D, D], F32, kind="ExternalInput")
    gam_d = nc.dram_tensor("gam", [1, D], F32, kind="ExternalInput")
    bet_d = nc.dram_tensor("bet", [1, D], F32, kind="ExternalInput")
    mask_d = nc.dram_tensor("mask01", [128, 128], BF16, kind="ExternalInput")
    idf_d = nc.dram_tensor("idf", [128, 128], F32, kind="ExternalInput")
    idb_d = nc.dram_tensor("idb", [128, 128], BF16, kind="ExternalInput")
    onesr_d = nc.dram_tensor("onesr", [1, 128], F32, kind="ExternalInput")
    y_d = nc.dram_tensor("y", [TPC, D], F32, kind="ExternalOutput")
    x2s_d = nc.dram_tensor("x2s", [TPC, D], F32, kind="Internal")
    xn2s_d = nc.dram_tensor("xn2s", [TPC, D], F32, kind="Internal")
    cc_in_d = nc.dram_tensor("cc_in", [1, 16], F32, kind="Internal")
    cc_out_d = nc.dram_tensor("cc_out", [1, 16], F32, kind="Internal",
                              addr_space="Shared")

    A = nc.alloc_sbuf_tensor
    mask_sb = A("mask_sb", [128, 128], BF16)
    idf_sb = A("idf_sb", [128, 128], F32)
    idb_sb = A("idb_sb", [128, 128], BF16)
    onesr_sb = A("onesr_sb", [1, 128], F32)
    gam_sb = A("gam_sb", [1, D], F32)
    bet_sb = A("bet_sb", [1, D], F32)
    gb_sb = A("gb_sb", [128, D], F32)
    bb_sb = A("bb_sb", [128, D], F32)
    wqt_sb = A("wqt_sb", [128, 8 * D], BF16)
    bn12 = A("bn12", [128, 12], F32)
    bn2 = A("bn2", [128, 2], F32)
    vp = A("vp", [128, 1], F32)
    iv = A("iv", [128, 1], F32)
    rr = A("rr", [128, 1], F32)
    bcol = A("bcol", [128, 1], F32)
    bn12b = A("bn12b", [128, 12], F32)
    bn2b = A("bn2b", [128, 2], F32)
    vpb = A("vpb", [128, 1], F32)
    ivb = A("ivb", [128, 1], F32)
    rrb = A("rrb", [128, 1], F32)
    bcolb = A("bcolb", [128, 1], F32)
    mxcol = A("mxcol", [128, 1], F32)
    tmax = A("tmax", [128, 1], F32)
    tmax16 = A("tmax16", [128, 16], F32)
    SA = [dict(s6=A(f"sa6_{i}", [128, 12], F32), s2=A(f"sa2_{i}", [128, 2], F32),
               vp=A(f"savp_{i}", [128, 1], F32), iv=A(f"saiv_{i}", [128, 1], F32),
               rt=A(f"sar_{i}", [128, 1], F32), bt=A(f"sab_{i}", [128, 1], F32))
          for i in range(2)]
    SB = [dict(s6=A(f"sb6_{i}", [128, 12], F32), s2=A(f"sb2_{i}", [128, 2], F32),
               vp=A(f"sbvp_{i}", [128, 1], F32), iv=A(f"sbiv_{i}", [128, 1], F32),
               rt=A(f"sbr_{i}", [128, 1], F32), bt=A(f"sbb_{i}", [128, 1], F32))
          for i in range(2)]
    rc_sb = A("rc_sb", [128, 7], F32)
    ones_col = A("ones_col", [128, 1], F32)
    ws8 = A("ws8", [128, 8], F32)
    wm8 = A("wm8", [128, 8], F32)
    wc8 = A("wc8", [128, 8], F32)
    dcol = A("dcol", [128, 1], F32)
    ndcol = A("ndcol", [128, 1], F32)
    srow = A("srow", [1, 12], F32)
    gmr = A("gmr", [1, 4], F32)
    alpha = A("alpha", [1, 1], F32)
    invs_col = A("invs_col", [128, 1], F32)
    sa_col = A("sa_col", [128, 1], F32)

    AR, off = {}, 0

    def region(name, size):
        nonlocal off
        AR[name] = off
        off += (size + 31) & ~31

    region("xT", 8192)
    region("xn", 32768)
    region("wv0", 16384)
    region("wv1", 16384)
    region("K", 32768)
    region("V", 33280)
    region("Q", 32768)
    ARENA = off
    arena = A("arena", [128, ARENA], U8)
    abase = nc.lookup_mloc(arena).addr

    def place(name, shape, dtype, reg, extra=0):
        hnd = nc.virtual_sbuf_tensor(name, list(shape), dtype)
        nc._tensor(hnd.name, list(hnd.shape), hnd.dtype, type="SB")
        mloc = nc.lookup_mloc(hnd)
        mloc.allocated = True
        mloc.addr = abase + AR[reg] + extra
        return hnd

    xT = [place(f"xT{i}", [128, 1024], F32, "xT", 4096 * i) for i in range(2)]
    xn_fm = place("xn_fm", [128, 16384], BF16, "xn")
    attn_tm = place("attn_tm", [128, 16384], BF16, "xn")
    wv0 = place("wv0", [128, 8192], BF16, "wv0")
    wo_sb = place("wo_sb", [128, 8192], BF16, "wv0")
    wv1 = place("wv1", [128, 8192], BF16, "wv1")
    E_sb = place("E_sb", [128, 6144], BF16, "wv1")
    ternB = place("ternB", [128, 1024], BF16, "wv1", 12288)
    ternB2 = place("ternB2", [128, 1024], BF16, "K", 28672)
    yO2 = [place("yO", [128, 1024], F32, "wv1", 0),
           place("yO1", [128, 1024], F32, "Q", 0)]
    hO2 = [place("hO", [128, 1024], F32, "wv1", 4096),
           place("hO1", [128, 1024], F32, "Q", 4096)]
    K_fm = place("K_fm", [128, 16384], BF16, "K")
    x2O = [place(f"x2O{i}", [128, 1024], F32, "K", 4096 * i) for i in range(2)]
    wfb = [place(f"wfb{i}", [128, 1024], F32, "K", 8192 + 4096 * i) for i in range(2)]
    g1W = place("g1W", [128, 1024], F32, "K", 16384)
    g2W = place("g2W", [128, 1024], F32, "K", 20480)
    ternW = place("ternW", [128, 1024], F32, "K", 24576)
    t1O2 = [place("t1O", [128, 1024], F32, "K", 28672),
            place("t1O1", [128, 1024], F32, "Q", 8192)]
    V_aug = place("V_aug", [128, 16640], BF16, "V")
    attn_fm = place("attn_fm", [128, 16384], BF16, "V")
    Q_fm = place("Q_fm", [128, 16384], BF16, "Q")
    xnC = [place(f"xnC{i}", [128, 1024], F32, "Q", 4096 * i) for i in range(2)]
    t1C = place("t1C", [128, 1024], F32, "Q", 8192)
    intsT = [place(f"intsT{i}", [128, 1024], BF16, "Q", 12288 + 2048 * i) for i in range(2)]
    intsF = [place(f"intsF{i}", [128, 1024], BF16, "Q", 16384 + 2048 * i) for i in range(2)]
    outC = [place(f"outC{i}", [128, 1024], F32, "Q", 20480 + 4096 * i) for i in range(2)]

    psS = [nc.alloc_psum_tensor(f"psS{i}", [128, 1024], F32) for i in range(2)]
    psPV = [nc.alloc_psum_tensor(f"psPV{i}", [128, 512], F32) for i in range(3)]
    psX = nc.alloc_psum_tensor("psX", [128, 512], F32)

    s_in = Sem(nc, "s_in")
    s_w = Sem(nc, "s_w")
    s_tr = Sem(nc, "s_tr")
    s_mm = Sem(nc, "s_mm")
    s_v = Sem(nc, "s_v")
    s_a = Sem(nc, "s_a")
    s_st = Sem(nc, "s_st")
    s_cc = Sem(nc, "s_cc")
    s_g = Sem(nc, "s_g")
    s_y = Sem(nc, "s_y")

    sched = {k: [] for k in ("sync", "tensor", "vector", "scalar", "gpsimd")}

    def emit(eng, fn, waits=(), incs=()):
        waits = tuple((s.h, int(v)) for (s, v) in waits if v > 0)
        incs = tuple((s.h, int(k)) for (s, k) in incs)

        def run(e, fn=fn, waits=waits, incs=incs):
            for h, v in waits:
                e.wait_ge(h, v)
            inst = fn(e)
            for h, k in incs:
                inst.then_inc(h, k)
        sched[eng].append(run)

    def w(sem):
        return (sem, sem.n)

    def vwait(sem):
        return [(sem, sem.n)]

    tp_state = {"g": 0, "hist": []}

    def transpose_blocks(src_fn, dst_fn, n, first_waits, evac="vector"):
        first = True
        for g0 in range(0, n, 4):
            cnt = min(4, n - g0)
            buf = psX if tp_state["g"] % 2 == 0 else psPV[0]
            tp_state["g"] += 1
            guard = tp_state["hist"][-2] if len(tp_state["hist"]) >= 2 else None
            for j in range(g0, g0 + cnt):
                wl = []
                if j == g0:
                    wl = [guard] if guard else []
                    if first:
                        wl += list(first_waits)
                        first = False
                emit("tensor", lambda e, j=j, g0=g0, buf=buf: e.matmul(
                    buf[:, 128 * (j - g0):128 * (j - g0) + 128],
                    src_fn(j), idb_sb[:, :], start=True, stop=True),
                    waits=wl, incs=([s_mm.bump()] if j == g0 + cnt - 1 else []))
            if evac == "vector":
                emit("vector", lambda e, g0=g0, cnt=cnt, buf=buf: e.tensor_copy(
                    dst_fn(g0, cnt),
                    buf.ap()[:, 0:128 * cnt].rearrange("p (b r) -> p b r", r=128)),
                    waits=vwait(s_mm), incs=[s_v.bump()])
                tp_state["hist"].append((s_v, s_v.n))
            else:
                emit("scalar", lambda e, g0=g0, cnt=cnt, buf=buf: e.activation(
                    dst_fn(g0, cnt),
                    buf.ap()[:, 0:128 * cnt].rearrange("p (b r) -> p b r", r=128),
                    AF.Identity, bias=np.float32(0.0), scale=np.float32(1.0)),
                    waits=vwait(s_mm), incs=[s_a.bump()])
                tp_state["hist"].append((s_a, s_a.n))
        return tp_state["hist"][-1]

    for dst, src in ((mask_sb, mask_d), (idf_sb, idf_d), (idb_sb, idb_d),
                     (onesr_sb, onesr_d), (gam_sb, gam_d), (bet_sb, bet_d)):
        emit("sync", lambda e, d=dst, s=src: e.dma_start(out=d[:, :], in_=s[:, :]),
             incs=[s_in.bump(16)])
    consts_in = s_in.n
    for t, val in ((ones_col, 1.0), (mxcol, 0.0)):
        emit("vector", lambda e, t=t, v=val: e.memset(t[:, :], v))
    emit("vector", lambda e: e.memset(
        V_aug.ap().rearrange("p (t r) -> p t r", t=16)[:, :, 64:1040:65], 1.0))
    emit("vector", lambda e: e.drain(), incs=[s_v.bump()])
    memset_v = s_v.n
    for which, (src, dst) in enumerate(((gam_sb, gb_sb), (bet_sb, bb_sb))):
        for hf in range(2):
            emit("tensor", lambda e, s=src, hf=hf: e.matmul(
                psX[:, 0:512], onesr_sb[:, :], s[0:1, 512 * hf:512 * hf + 512],
                start=True, stop=True),
                waits=[(s_in, consts_in)] + ([] if which == 0 and hf == 0 else vwait(s_v)),
                incs=[s_mm.bump()])
            emit("vector", lambda e, d=dst, hf=hf: e.tensor_copy(
                d[:, 512 * hf:512 * hf + 512], psX[:, 0:512]),
                waits=vwait(s_mm), incs=[s_v.bump()])
    gbeta_v = s_v.n

    wkv3 = wkv_d.ap().rearrange("(c p) i -> p c i", p=128)
    wq3 = wq_d.ap().rearrange("(c p) i -> p c i", p=128)
    wo3 = wo_d.ap().rearrange("(c p) i -> p c i", p=128)
    wf3 = wf_d.ap().rearrange("(c p) i -> p c i", p=128)

    def stats_part1(src_ap, S, extra_waits=(), eps_mul=None):
        s6, s2, vpt, ivt, rt = (S[k] for k in ("s6", "s2", "vp", "iv", "rt"))
        emit("vector", lambda e, a=s6, b=src_ap: e.bn_stats(a[:, 0:6], b[:, 0:512]),
             waits=list(extra_waits))
        emit("vector", lambda e, a=s6, b=src_ap: e.bn_stats(a[:, 6:12], b[:, 512:1024]))
        emit("vector", lambda e: e.drain())
        emit("vector", lambda e, a=s2, b=s6: e.bn_aggr(a[:, :], b[:, :]))
        emit("vector", lambda e: e.drain())
        if eps_mul is None:
            emit("vector", lambda e, a=vpt, b=s2: e.tensor_scalar(
                a[:, :], b[:, 1:2], EPS, None, ALU.add))
        else:
            m, cadd = eps_mul
            emit("vector", lambda e, a=vpt, b=s2, m=m, c=cadd: e.tensor_scalar(
                a[:, :], b[:, 1:2], m, c, ALU.mult, ALU.add))
        emit("vector", lambda e: e.drain())
        emit("vector", lambda e, a=ivt, b=vpt: e.reciprocal(a[:, :], b[:, :]),
             incs=[s_v.bump()])
        emit("scalar", lambda e, a=rt, b=ivt: e.activation(a[:, :], b[:, :], AF.Sqrt),
             waits=vwait(s_v), incs=[s_a.bump()])
        return s_a.n

    def stats_part2(S, sqrt_a):
        s2, rt, bt = S["s2"], S["rt"], S["bt"]
        emit("vector", lambda e, a=bt, b=s2, c=rt: e.scalar_tensor_tensor(
            a[:, :], b[:, 0:1], -1.0, c[:, :], ALU.mult, ALU.mult),
            waits=[(s_a, sqrt_a)])
        emit("vector", lambda e: e.drain(), incs=[s_v.bump()])

    def stats_block2(src_ap, S, extra_waits=()):
        sa = stats_part1(src_ap, S, extra_waits)
        stats_part2(S, sa)

    def stats_block(src_ap, s6, s2, vpt, ivt, rt, bt, extra_waits=()):
        stats_block2(src_ap, dict(s6=s6, s2=s2, vp=vpt, iv=ivt, rt=rt, bt=bt),
                     extra_waits)

    o_evac_last = [None]
    for sq in range(2):
        tbase = NTS * sq

        vec_now = s_v.n
        emit("sync", lambda e, t=tbase: e.dma_start(
            out=xT[0][:, :], in_=x_d[128 * t:128 * (t + 1), :]),
            waits=[(s_v, vec_now)], incs=[s_in.bump(16)])
        emit("gpsimd", lambda e: e.dma_start(
            out=wv0.ap().rearrange("p (c i) -> p c i", c=8),
            in_=wkv3[:, :, 0:1024]),
            waits=[(s_mm, s_mm.n)], incs=[s_w.bump(16)])
        wk_ready = s_w.n

        evac_hist = []
        base_v = s_v.n
        base_st = s_st.n
        base_a = s_a.n
        slot_ctr = [0]

        def emit_proj_group(dst, w_sb, w_ready_val, tb, kc, extra):
            slot = slot_ctr[0]
            slot_ctr[0] += 1
            p = psS[slot % 2][:, 512 * ((slot // 2) % 2):
                              512 * ((slot // 2) % 2) + 512]
            guard = evac_hist[-4] if len(evac_hist) >= 4 else base_v
            for c in range(8):
                wl = []
                if c == 0:
                    wl = [(s_w, w_ready_val), (s_v, guard)] + extra
                emit("tensor", lambda e, kc=kc, tb=tb, c=c, p=p, ws=w_sb: e.matmul(
                    p, ws[:, 1024 * c + 128 * kc:1024 * c + 128 * kc + 128],
                    xn_fm[:, 2048 * c + 512 * tb:2048 * c + 512 * tb + 512],
                    start=(c == 0), stop=(c == 7)),
                    waits=wl, incs=([s_mm.bump()] if c == 7 else []))
            emit("vector", lambda e, kc=kc, tb=tb, p=p, d=dst: e.tensor_copy(
                d[:, 2048 * kc + 512 * tb:2048 * kc + 512 * tb + 512], p),
                waits=vwait(s_mm)
                + ([(s_st, base_st), (s_a, base_a)]
                   if len(evac_hist) == 0 else []),
                incs=[s_v.bump()])
            evac_hist.append(s_v.n)

        kq = [(tb, kc) for tb in range(4) for kc in range(8)]
        kq_i = [0]
        tr_ev_tile = {}
        xn_act_hist = {}
        tr_hist = {}
        for tt in range(NTS):
            buf = tt % 2
            xt_wait = s_in.n
            if tt + 1 < NTS:
                g = xn_act_hist.get((tt + 1) % 2, 0)
                emit("sync", lambda e, t=tbase + tt + 1, b=(tt + 1) % 2: e.dma_start(
                    out=xT[b][:, :], in_=x_d[128 * t:128 * (t + 1), :]),
                    waits=[(s_a, g), (s_v, vec_now)], incs=[s_in.bump(16)])
            stats_block(xT[buf], bn12, bn2, vp, iv, rr, bcol,
                        extra_waits=[(s_in, xt_wait)]
                        + ([(s_v, gbeta_v)] if tt == 0 and sq == 0 else []))
            emit("scalar", lambda e, b=buf: e.activation(
                E_sb[:, 2048 * b:2048 * b + 1024], xT[b][:, :], AF.Identity,
                bias=bcol[:, :], scale=rr[:, :]),
                waits=vwait(s_v) + ([(s_mm, tr_hist[buf])] if buf in tr_hist else [])
                + ([(s_g, s_g.n)] if tt == 0 else []),
                incs=[s_a.bump()])
            xn_act_hist[buf] = s_a.n
            tr_ev_tile[tt] = transpose_blocks(
                lambda j, b=buf: E_sb[:, 2048 * b + 128 * j:2048 * b + 128 * j + 128],
                lambda g0, cnt, t=tt: xn_fm.ap().rearrange(
                    "p (c r) -> p c r", c=8)[:, g0:g0 + cnt,
                                             128 * t:128 * t + 128],
                8, [(s_a, xn_act_hist[buf])])
            tr_hist[buf] = s_mm.n
            ready = 8 * ((tt - 3) // 4 + 1) if tt >= 3 else 0
            budget = 3 if sq == 1 else 0
            while budget > 0 and kq_i[0] < ready:
                tb, kc = kq[kq_i[0]]
                emit_proj_group(K_fm, wv0, wk_ready, tb, kc,
                                [tr_ev_tile[4 * tb + 3]] if kc == 0 else [])
                kq_i[0] += 1
                budget -= 1
        while kq_i[0] < len(kq):
            tb, kc = kq[kq_i[0]]
            emit_proj_group(K_fm, wv0, wk_ready, tb, kc,
                            [tr_ev_tile[4 * tb + 3]] if kc == 0 else [])
            kq_i[0] += 1
        xn_tr_v = s_v.n

        emit("gpsimd", lambda e: e.dma_start(
            out=wv1.ap().rearrange("p (c i) -> p c i", c=8),
            in_=wkv3[:, :, 1024:2048]),
            waits=[tr_ev_tile[NTS - 1]], incs=[s_w.bump(16)])
        wv_ready = s_w.n

        emit("gpsimd", lambda e: e.dma_start(
            out=wv0.ap().rearrange("p (c i) -> p c i", c=8), in_=wq3),
            waits=[(s_mm, s_mm.n)], incs=[s_w.bump(16)])
        wq_ready = s_w.n

        emit("vector", lambda e: e.memset(
            V_aug.ap().rearrange("p (t r) -> p t r", t=16)[:, :, 64:1040:65], 1.0),
            waits=[(s_mm, s_mm.n)] + ([(s_v, memset_v)] if sq == 0 else [])
            + ([o_evac_last[0]] if o_evac_last[0] else []))
        emit("vector", lambda e: e.drain(), incs=[s_v.bump()])
        ones_v = s_v.n
        for tt in range(NTS):
            p = psS[tt % 2]
            guard = evac_hist[-2] if len(evac_hist) >= 2 else base_v
            for c in range(8):
                for hf in range(2):
                    wl = []
                    if c == 0 and hf == 0:
                        wl = [(s_w, wv_ready), (s_v, guard)]
                    emit("tensor", lambda e, t=tt, c=c, hf=hf, p=p: e.matmul(
                        p[:, 512 * hf:512 * hf + 512],
                        xn_fm[:, 2048 * c + 128 * t:2048 * c + 128 * t + 128],
                        wv1[:, 1024 * c + 512 * hf:1024 * c + 512 * hf + 512],
                        start=(c == 0), stop=(c == 7)),
                        waits=wl,
                        incs=([s_mm.bump()] if c == 7 and hf == 1 else []))
            emit("vector", lambda e, t=tt, p=p: e.tensor_copy(
                V_aug[:, 1040 * t:1040 * t + 1040]
                     .rearrange("p (h r) -> p h r", r=65)[:, :, 0:64],
                p.ap().rearrange("p (h r) -> p h r", h=16)),
                waits=vwait(s_mm) + ([(s_v, ones_v)] if tt == 0 else []),
                incs=[s_v.bump()])
            evac_hist.append(s_v.n)

        for tb in range(4):
            for kc in range(8):
                emit_proj_group(Q_fm, wv0, wq_ready, tb, kc, [])
        q_done_v = s_v.n
        q_done_mm = s_mm.n

        emit("gpsimd", lambda e: e.dma_start(
            out=wo_sb.ap().rearrange("p (c i) -> p c i", c=8), in_=wo3),
            waits=[(s_mm, q_done_mm)], incs=[s_w.bump(16)])
        wo_ready = s_w.n

        steps = []
        for h in range(H):
            for pas in range(2):
                for jj in (range(8) if pas == 0 else range(16)):
                    steps.append((h, pas, jj))
        exp_cnt = []
        mm_step = []
        pv_ev = s_v.n
        head_started_banks = {}
        head_first_pv = {}

        def emit_S(k):
            h, pas, jj = steps[k]
            hc, po = h // 2, 64 * (h % 2)
            ps = psS[k % 2]
            qlo = max(128 * jj, 1024 * pas)
            qhi = 1024 * (pas + 1)
            q0 = qlo
            first = True
            while q0 < qhi:
                q1 = min(q0 + 512, qhi,
                         1024 * pas + ((q0 - 1024 * pas) // 512 + 1) * 512)
                wl = []
                if first:
                    if k >= 2:
                        wl.append((s_a, exp_cnt[k - 2]))
                    else:
                        wl.append((s_v, q_done_v))
                    first = False
                emit("tensor", lambda e, q0=q0, q1=q1, p=ps, hc=hc,
                     po=po, jj=jj, pas=pas: e.matmul(
                         p[:, q0 - 1024 * pas:q1 - 1024 * pas],
                         K_fm[po:po + 64,
                              2048 * hc + 128 * jj:2048 * hc + 128 * jj + 128],
                         Q_fm[po:po + 64, 2048 * hc + q0:2048 * hc + q1],
                         start=True, stop=True),
                     waits=wl,
                     incs=([s_mm.bump()] if q1 == qhi else []))
                q0 = q1
            mm_step.append(s_mm.n)

        msk_of = {}

        def emit_exp_mask(k):
            h, pas, jj = steps[k]
            ps = psS[k % 2]
            eoff = 2048 * (k % 3)
            qlo = max(128 * jj, 1024 * pas)
            qhi = 1024 * (pas + 1)
            emit("scalar", lambda e, qlo=qlo, qhi=qhi, p=ps, eoff=eoff,
                 pas=pas: e.activation(
                     E_sb[:, eoff + qlo - 1024 * pas:eoff + qhi - 1024 * pas],
                     p[:, qlo - 1024 * pas:qhi - 1024 * pas],
                     AF.Exp, bias=np.float32(EXP_C), scale=np.float32(0.125)),
                 waits=[(s_mm, mm_step[k])], incs=[s_a.bump()])
            exp_cnt.append(s_a.n)
            if 1024 * pas <= 128 * jj < 1024 * (pas + 1):
                emit("vector", lambda e, jj=jj, eoff=eoff, pas=pas: e.tensor_tensor(
                    E_sb[:, eoff + 128 * jj - 1024 * pas:
                         eoff + 128 * jj - 1024 * pas + 128],
                    E_sb[:, eoff + 128 * jj - 1024 * pas:
                         eoff + 128 * jj - 1024 * pas + 128],
                    mask_sb[:, :], ALU.mult),
                    waits=[(s_a, exp_cnt[k])], incs=[s_v.bump()])
                msk_of[k] = s_v.n

        def emit_PV(k):
            nonlocal pv_ev
            h, pas, jj = steps[k]
            eoff = 2048 * (k % 3)
            ilo = max(jj, 8 * pas)
            ihi = 8 * (pas + 1)
            order = [i for i in range(ilo, ihi) if i != jj]
            if ilo <= jj < ihi:
                order.append(jj)
            started = head_started_banks.setdefault(h, set())
            for n_i, i in enumerate(order):
                bnk, sl = i // 7, i % 7
                wl = []
                if n_i == 0:
                    wl.append((s_a, exp_cnt[k]))
                    if not head_first_pv.get(h):
                        wl.append((s_v, pv_ev))
                        head_first_pv[h] = True
                if k in msk_of and i == jj:
                    wl.append((s_v, msk_of[k]))
                last = (jj == i)
                fin = last and (i in (6, 13, 15))
                st_flag = bnk not in started
                if st_flag:
                    started.add(bnk)
                emit("tensor", lambda e, i=i, jj=jj, h=h, eoff=eoff,
                     bnk=bnk, sl=sl, pas=pas, last=last, st=st_flag: e.matmul(
                         psPV[bnk][:, 65 * sl:65 * sl + 65],
                         E_sb[:, eoff + 128 * i - 1024 * pas:
                              eoff + 128 * i - 1024 * pas + 128],
                         V_aug[:, 1040 * jj + 65 * h:1040 * jj + 65 * h + 65],
                         start=st, stop=last),
                     waits=wl,
                     incs=([s_mm.bump()] if fin else []))
            if pas == 1 and jj == 15:
                base_mm = s_mm.n - 3
                for bnk, ntile in ((0, 7), (1, 7), (2, 2)):
                    emit("vector", lambda e, bnk=bnk, nt=ntile: e.reciprocal(
                        rc_sb[:, 0:nt], psPV[bnk][:, 64:64 + (nt - 1) * 65 + 1:65]),
                        waits=[(s_mm, base_mm + bnk + 1)])
                    emit("vector", lambda e: e.drain())
                    emit("vector", lambda e, bnk=bnk, nt=ntile, h=h: e.tensor_tensor(
                        attn_tm.ap().rearrange("p (t r) -> p t r", r=1024)
                               [:, 7 * bnk:7 * bnk + nt, 64 * h:64 * h + 64],
                        psPV[bnk][:, 0:nt * 65].rearrange("p (t r) -> p t r", r=65)
                                [:, :, 0:64],
                        rc_sb[:, 0:nt].unsqueeze(2).broadcast_to([128, nt, 64]),
                        ALU.mult))
                    emit("vector", lambda e: e.drain(), incs=[s_v.bump()])
                pv_ev = s_v.n

        emit_S(0)
        for k in range(len(steps)):
            emit_exp_mask(k)
            if k + 1 < len(steps):
                emit_S(k + 1)
            if k >= 1:
                emit_PV(k - 1)
        emit_PV(len(steps) - 1)
        attn_v = s_v.n
        attn_mm = s_mm.n
        exp_last = exp_cnt[-1]

        vec_now = s_v.n
        emit("sync", lambda e, t=tbase: e.dma_start(
            out=xT[0][:, :], in_=x_d[128 * t:128 * (t + 1), :]),
            waits=[(s_a, xn_act_hist[0])], incs=[s_in.bump(16)])
        x2v_hist, apply_v, st_hist, gh_hist = {}, {}, {}, {}
        h_v, mx_v_of, pend = {}, {}, {}
        mx_v_last = [0]
        tr_v_of = {}

        def o_transposes(t2, first):
            tr_v_of[t2] = transpose_blocks(
                lambda j, t=t2: attn_tm[:, 1024 * t + 128 * j:1024 * t + 128 * j + 128],
                lambda g0, cnt, t=t2: attn_fm.ap().rearrange(
                    "p (c r) -> p c r", c=8)[:, g0:g0 + cnt,
                                             128 * t:128 * t + 128],
                8, [(s_v, attn_v)] if first else [], evac="scalar")

        def emit_redmax(pt):
            b = pt % 2
            emit("vector", lambda e, b=b, pt=pt: e.reduce_max(
                tmax16[:, pt:pt + 1], t1O2[b][:, :], axis=AX.X,
                apply_absolute_value=True),
                waits=[(s_g, gh_hist[(pt, "g")])], incs=[s_v.bump()])
            mx_v_of[b] = s_v.n
            mx_v_last[0] = s_v.n

        def apply_unit(pt):
            b = pt % 2
            Sa = SA[b]
            sa_a = pend[b]
            stats_part2(Sa, sa_a)
            emit("vector", lambda e, b=b, Sa=Sa: e.tensor_scalar(
                hO2[b][:, :], x2O[b][:, :], Sa["rt"][:, :], Sa["bt"][:, :],
                ALU.mult, ALU.add),
                waits=[(s_g, gh_hist.get(b, 0))], incs=[s_v.bump()])
            h_v[b] = apply_v[b] = s_v.n
            emit("gpsimd", lambda e, b=b: e.tensor_mul(
                t1O2[b][:, :], hO2[b][:, :], gb_sb[:, :]),
                waits=[(s_v, h_v[b]), (s_st, st_hist.get(("xn2", b), 0)),
                       (s_v, mx_v_of.get(b, 0))])
            emit("gpsimd", lambda e, b=b: e.tensor_add(
                t1O2[b][:, :], t1O2[b][:, :], bb_sb[:, :]), incs=[s_g.bump()])
            gh_hist[b] = gh_hist[(pt, "g")] = s_g.n
            emit("sync", lambda e, t=tbase + pt, b=b: e.dma_start(
                out=xn2s_d[128 * t:128 * (t + 1), :], in_=t1O2[b][:, :]),
                waits=[(s_g, gh_hist[b])], incs=[s_st.bump(16)])
            st_hist[("xn2", b)] = s_st.n
            if pt > 0:
                emit_redmax(pt - 1)

        o_transposes(0, True)
        for tt in range(NTS):
            buf = tt % 2
            xt_wait = s_in.n
            if tt + 1 < NTS:
                g = x2v_hist.get((tt + 1) % 2, 0)
                ga = xn_act_hist[(tt + 1) % 2] if tt == 0 else 0
                emit("sync", lambda e, t=tbase + tt + 1, b=(tt + 1) % 2,
                     g=g, ga=ga: e.dma_start(
                         out=xT[b][:, :], in_=x_d[128 * t:128 * (t + 1), :]),
                     waits=[(s_v, g), (s_a, ga)], incs=[s_in.bump(16)])
                o_transposes(tt + 1, False)
            p = psS[tt % 2]
            for c in range(8):
                for hf in range(2):
                    wl = []
                    if c == 0 and hf == 0:
                        wl = [tr_v_of[tt], (s_w, wo_ready)]
                        if tt < 2:
                            wl.append((s_a, exp_last))
                        else:
                            wl.append((s_v, x2v_hist[tt % 2]))
                    emit("tensor", lambda e, t=tt, c=c, hf=hf, p=p: e.matmul(
                        p[:, 512 * hf:512 * hf + 512],
                        attn_fm[:, 2048 * c + 128 * t:2048 * c + 128 * t + 128],
                        wo_sb[:, 1024 * c + 512 * hf:1024 * c + 512 * hf + 512],
                        start=(c == 0), stop=(c == 7)),
                        waits=wl,
                        incs=([s_mm.bump()] if c == 7 and hf == 1 else []))
            emit("vector", lambda e, b=buf, p=p: e.tensor_add(
                x2O[b][:, :], p[:, :], xT[b][:, :]),
                waits=vwait(s_mm) + [(s_in, xt_wait),
                                     (s_st, st_hist.get(buf, 0))])
            emit("vector", lambda e: e.drain(), incs=[s_v.bump()])
            x2v_hist[buf] = s_v.n
            emit("sync", lambda e, t=tbase + tt, b=buf: e.dma_start(
                out=x2s_d[128 * t:128 * (t + 1), :], in_=x2O[b][:, :]),
                waits=[(s_v, x2v_hist[buf])], incs=[s_st.bump(16)])
            st_hist[buf] = s_st.n
            pend[buf] = stats_part1(x2O[buf], SA[buf],
                                    eps_mul=(1.0 + EPS, EPS * EPS))
            if tt > 0:
                apply_unit(tt - 1)
        apply_unit(NTS - 1)
        emit_redmax(NTS - 1)
        emit("vector", lambda e: e.drain())
        emit("vector", lambda e: e.reduce_max(tmax[:, :], tmax16[:, :], axis=AX.X))
        emit("vector", lambda e: e.drain())
        emit("vector", lambda e: e.tensor_max(mxcol[:, :], mxcol[:, :], tmax[:, :]))
        emit("vector", lambda e: e.drain(), incs=[s_v.bump()])
        mx_v_last[0] = s_v.n
        o_evac_last[0] = tr_v_of[NTS - 1]

        if sq == 0:
            emit("gpsimd", lambda e: e.dma_start(out=wfb[0][:, :], in_=wf3[:, 0, :]),
                 waits=[(s_v, attn_v)], incs=[s_w.bump(16)])
            wf_rd = {}
            for c in range(8):
                wf_wait = s_w.n
                if c + 1 < 8:
                    emit("gpsimd", lambda e, c=c: e.dma_start(
                        out=wfb[(c + 1) % 2][:, :], in_=wf3[:, c + 1, :]),
                        waits=[(s_v, wf_rd.get((c + 1) % 2, 0))],
                        incs=[s_w.bump(16)])
                emit("vector", lambda e, c=c: e.reduce_sum(
                    ws8[:, c:c + 1], wfb[c % 2][:, :], axis=AX.X,
                    apply_absolute_value=True),
                    waits=[(s_w, wf_wait)], incs=[s_v.bump()])
                wf_rd[c % 2] = s_v.n
            emit("tensor", lambda e: e.matmul(
                psX[0:1, 0:8], ones_col[:, :], ws8[:, :], start=True, stop=True),
                waits=vwait(s_v), incs=[s_mm.bump()])
            emit("vector", lambda e: e.reduce_sum(srow[0:1, 0:1], psX[0:1, 0:8],
                                                  axis=AX.X),
                 waits=vwait(s_mm))
            emit("vector", lambda e: e.drain())
            emit("vector", lambda e: e.tensor_scalar(
                srow[0:1, 1:2], srow[0:1, 0:1], 0.7 / (1024.0 * 1024.0), None, ALU.mult))
            emit("vector", lambda e: e.drain(), incs=[s_v.bump()])
            emit("tensor", lambda e: e.matmul(
                psX[:, 128:129], onesr_sb[:, :], srow[0:1, 1:2], start=True, stop=True),
                waits=vwait(s_v), incs=[s_mm.bump()])
            emit("vector", lambda e: e.tensor_copy(dcol[:, :], psX[:, 128:129]),
                 waits=vwait(s_mm))
            emit("vector", lambda e: e.drain())
            emit("vector", lambda e: e.tensor_scalar(ndcol[:, :], dcol[:, :], -1.0,
                                                     None, ALU.mult))
            emit("vector", lambda e: e.drain(), incs=[s_v.bump()])
            dcol_v = s_v.n
            emit("gpsimd", lambda e: e.dma_start(out=wfb[0][:, :], in_=wf3[:, 0, :]),
                 waits=[(s_v, dcol_v)], incs=[s_w.bump(16)])
            tr_mm_of = {}
            tern_a_of = {}
            wf_rd = {}
            tB = [ternB, ternB2]
            st_xn2_0 = st_hist.get(("xn2", 0), 0)
            for c in range(8):
                wf_wait = s_w.n
                if c + 1 < 8:
                    emit("gpsimd", lambda e, c=c: e.dma_start(
                        out=wfb[(c + 1) % 2][:, :], in_=wf3[:, c + 1, :]),
                        waits=[(s_v, wf_rd.get((c + 1) % 2, 0))],
                        incs=[s_w.bump(16)])
                b = c % 2
                emit("vector", lambda e, b=b: e.tensor_scalar(
                    g1W[:, :], wfb[b][:, :], dcol[:, :], None, ALU.is_gt),
                    waits=[(s_w, wf_wait)])
                emit("vector", lambda e, b=b: e.tensor_scalar(
                    g2W[:, :], wfb[b][:, :], ndcol[:, :], None, ALU.is_lt))
                emit("vector", lambda e: e.drain())
                emit("vector", lambda e: e.tensor_sub(ternW[:, :], g1W[:, :], g2W[:, :]),
                     waits=([(s_a, tern_a_of[c - 1])] if c > 0 else []))
                emit("vector", lambda e: e.drain(), incs=[s_v.bump()])
                tern_w_v = s_v.n
                emit("vector", lambda e, c=c: e.reduce_sum(
                    wc8[:, c:c + 1], ternW[:, :], axis=AX.X,
                    apply_absolute_value=True))
                emit("vector", lambda e, b=b: e.tensor_mul(g1W[:, :], wfb[b][:, :],
                                                           ternW[:, :]))
                emit("vector", lambda e: e.drain())
                emit("vector", lambda e, c=c: e.reduce_sum(
                    wm8[:, c:c + 1], g1W[:, :], axis=AX.X), incs=[s_v.bump()])
                wf_rd[b] = s_v.n
                emit("scalar", lambda e, b=b: e.activation(
                    tB[b][:, :], ternW[:, :], AF.Identity,
                    bias=np.float32(0.0), scale=np.float32(1.0)),
                    waits=[(s_v, tern_w_v)]
                    + ([(s_mm, tr_mm_of[c - 2])] if c > 1 else [])
                    + ([(s_st, st_xn2_0)] if c == 1 else []),
                    incs=[s_a.bump()])
                tern_a_of[c] = s_a.n
                transpose_blocks(
                    lambda ic, b=b: tB[b][:, 128 * ic:128 * ic + 128],
                    lambda g0, cnt, c=c: wqt_sb.ap().rearrange(
                        "p (ic r) -> p ic r", ic=8)[:, g0:g0 + cnt,
                                                    128 * c:128 * c + 128],
                    8, [(s_a, tern_a_of[c])], evac="scalar")
                tr_mm_of[c] = s_mm.n
            emit("tensor", lambda e: e.matmul(
                psX[0:1, 129:137], ones_col[:, :], wm8[:, :], start=True, stop=True),
                waits=vwait(s_v), incs=[s_mm.bump()])
            emit("tensor", lambda e: e.matmul(
                psX[0:1, 137:145], ones_col[:, :], wc8[:, :], start=False, stop=True),
                incs=[s_mm.bump()])
            emit("vector", lambda e: e.reduce_sum(
                srow[0:1, 8:9], psX[0:1, 129:137], axis=AX.X),
                waits=vwait(s_mm))
            emit("vector", lambda e: e.reduce_sum(
                srow[0:1, 9:10], psX[0:1, 137:145], axis=AX.X))
            emit("vector", lambda e: e.drain())
            emit("vector", lambda e: e.tensor_scalar(
                srow[0:1, 2:3], srow[0:1, 9:10], 1.0, None, ALU.max))
            emit("vector", lambda e: e.drain())
            emit("vector", lambda e: e.reciprocal(srow[0:1, 3:4], srow[0:1, 2:3]))
            emit("vector", lambda e: e.drain())
            emit("vector", lambda e: e.tensor_mul(alpha[0:1, :], srow[0:1, 8:9],
                                                  srow[0:1, 3:4]))
            emit("vector", lambda e: e.drain(), incs=[s_v.bump()])

    emit("tensor", lambda e: e.matmul(
        psX[0:1, 0:128], mxcol[:, :], idf_sb[:, :], start=True, stop=True),
        waits=vwait(s_v), incs=[s_mm.bump()])
    emit("vector", lambda e: e.reduce_max(gmr[0:1, 0:1], psX[0:1, 0:128], axis=AX.X),
         waits=vwait(s_mm))
    emit("vector", lambda e: e.drain(), incs=[s_v.bump()])
    emit("sync", lambda e: e.dma_start(out=cc_in_d[0:1, 0:1], in_=gmr[0:1, 0:1]),
         waits=vwait(s_v), incs=[s_st.bump(16)])
    emit("gpsimd", lambda e: e.collective_compute(
        "AllReduce", ALU.max, replica_groups=[list(range(N_CORES))],
        ins=[cc_in_d.ap().opt()], outs=[cc_out_d.ap().opt()]),
        waits=[w(s_st)], incs=[s_cc.bump(1)])
    emit("sync", lambda e: e.dma_start(out=gmr[0:1, 1:2], in_=cc_out_d[0:1, 0:1]),
         waits=[w(s_cc)], incs=[s_in.bump(16)])
    emit("vector", lambda e: e.tensor_scalar(gmr[0:1, 2:3], gmr[0:1, 1:2], 1e-8,
                                             None, ALU.max),
         waits=[w(s_in)])
    emit("vector", lambda e: e.drain())
    emit("vector", lambda e: e.reciprocal(gmr[0:1, 3:4], gmr[0:1, 2:3]))
    emit("vector", lambda e: e.drain())
    emit("vector", lambda e: e.tensor_scalar(srow[0:1, 4:5], gmr[0:1, 3:4], 127.0,
                                             None, ALU.mult))
    emit("vector", lambda e: e.tensor_scalar(srow[0:1, 5:6], gmr[0:1, 2:3], 1.0 / 127.0,
                                             None, ALU.mult))
    emit("vector", lambda e: e.drain())
    emit("vector", lambda e: e.tensor_mul(srow[0:1, 6:7], srow[0:1, 5:6], alpha[0:1, :]))
    emit("vector", lambda e: e.drain(), incs=[s_v.bump()])
    emit("tensor", lambda e: e.matmul(
        psX[:, 200:201], onesr_sb[:, :], srow[0:1, 4:5], start=True, stop=True),
        waits=vwait(s_v), incs=[s_mm.bump()])
    emit("tensor", lambda e: e.matmul(
        psX[:, 201:202], onesr_sb[:, :], srow[0:1, 6:7], start=False, stop=True),
        incs=[s_mm.bump()])
    emit("vector", lambda e: e.tensor_copy(invs_col[:, :], psX[:, 200:201]),
         waits=vwait(s_mm))
    emit("vector", lambda e: e.tensor_copy(sa_col[:, :], psX[:, 201:202]))
    emit("vector", lambda e: e.drain(), incs=[s_v.bump()])
    scale_v = s_v.n
    x2w_done = s_st.n

    emit("sync", lambda e: e.dma_start(out=xnC[0][:, :], in_=xn2s_d[0:128, :]),
         waits=[(s_st, x2w_done), (s_mm, s_mm.n)], incs=[s_in.bump(16)])
    emit("sync", lambda e: e.dma_start(out=xT[0][:, :], in_=x2s_d[0:128, :]),
         incs=[s_in.bump(16)])
    ints_hist, out_hist = {}, {}
    trC_hist, mmC_hist = {}, {}
    for tt in range(NT):
        buf = tt % 2
        in_wait = s_in.n
        if tt + 1 < NT:
            b2 = (tt + 1) % 2
            emit("sync", lambda e, t=tt + 1, b=b2, g=ints_hist.get(b2, 0): e.dma_start(
                out=xnC[b][:, :], in_=xn2s_d[128 * t:128 * (t + 1), :]),
                waits=[(s_v, g)], incs=[s_in.bump(16)])
            emit("sync", lambda e, t=tt + 1, b=b2, g=out_hist.get(b2, 0): e.dma_start(
                out=xT[b][:, :], in_=x2s_d[128 * t:128 * (t + 1), :]),
                waits=[(s_v, g)], incs=[s_in.bump(16)])
        emit("vector", lambda e, b=buf: e.tensor_scalar(
            t1C[:, :], xnC[b][:, :], invs_col[:, :], MAGIC, ALU.mult, ALU.add),
            waits=[(s_in, in_wait), (s_v, scale_v)]
            + ([(s_mm, trC_hist[buf][1])] if buf in trC_hist else []))
        emit("vector", lambda e: e.drain())
        emit("vector", lambda e, b=buf: e.tensor_scalar(
            intsT[b][:, :], t1C[:, :], -MAGIC, None, ALU.add))
        emit("vector", lambda e: e.drain(), incs=[s_v.bump()])
        ints_hist[buf] = s_v.n
        trC_hist[buf] = (transpose_blocks(
            lambda j, b=buf: intsT[b][:, 128 * j:128 * j + 128],
            lambda g0, cnt, b=buf: intsF[b][:, 128 * g0:128 * (g0 + cnt)]
                .rearrange("p (b2 r) -> p b2 r", r=128),
            8, [(s_v, ints_hist[buf])], evac="scalar"), s_mm.n)
        p = psS[tt % 2]
        for ic in range(8):
            for hf in range(2):
                wl = []
                if ic == 0 and hf == 0:
                    wl = [trC_hist[buf][0], (s_v, out_hist.get(buf, 0))]
                emit("tensor", lambda e, b=buf, ic=ic, hf=hf, p=p: e.matmul(
                    p[:, 512 * hf:512 * hf + 512],
                    intsF[b][:, 128 * ic:128 * ic + 128],
                    wqt_sb[:, 1024 * ic + 512 * hf:1024 * ic + 512 * hf + 512],
                    start=(ic == 0), stop=(ic == 7)),
                    waits=wl,
                    incs=([s_mm.bump()] if ic == 7 and hf == 1 else []))
        mmC_hist[buf] = s_mm.n
        emit("vector", lambda e, b=buf, p=p: e.scalar_tensor_tensor(
            outC[b][:, :], p[:, :], sa_col[:, :], xT[b][:, :], ALU.mult, ALU.add),
            waits=vwait(s_mm))
        emit("vector", lambda e: e.drain(), incs=[s_v.bump()])
        out_hist[buf] = s_v.n
        emit("sync", lambda e, t=tt, b=buf: e.dma_start(
            out=y_d[128 * t:128 * (t + 1), :], in_=outC[b][:, :]),
            waits=[(s_v, out_hist[buf])], incs=[s_y.bump(16)])
    emit("sync", lambda e: e.nop(), waits=[w(s_y)])

    with nc.Block() as blk:
        blk.sync(lambda e: [f(e) for f in sched["sync"]])
        blk.tensor(lambda e: [f(e) for f in sched["tensor"]])
        blk.vector(lambda e: [f(e) for f in sched["vector"]])
        blk.scalar(lambda e: [f(e) for f in sched["scalar"]])
        blk.gpsimd(lambda e: [f(e) for f in sched["gpsimd"]])
    return nc


_NC_CACHE = None


def kernel(L0_attn_x, L0_attn_Wq, L0_attn_Wkv, L0_attn_Wo,
           L0_ffn_weight, L0_ffn_gamma, L0_ffn_beta):
    global _NC_CACHE
    x = np.ascontiguousarray(np.asarray(L0_attn_x, dtype=np.float32))
    wq = np.ascontiguousarray(np.asarray(L0_attn_Wq, dtype=np.float32))
    wkv = np.ascontiguousarray(np.asarray(L0_attn_Wkv, dtype=np.float32))
    wo = np.ascontiguousarray(np.asarray(L0_attn_Wo, dtype=np.float32))
    wf = np.ascontiguousarray(np.asarray(L0_ffn_weight, dtype=np.float32))
    gam = np.ascontiguousarray(np.asarray(L0_ffn_gamma, dtype=np.float32)).reshape(1, D)
    bet = np.ascontiguousarray(np.asarray(L0_ffn_beta, dtype=np.float32)).reshape(1, D)
    mask01 = np.triu(np.ones((128, 128), dtype=np.float32)).astype(ml_dtypes.bfloat16)
    idf = np.eye(128, dtype=np.float32)
    idb = np.eye(128, dtype=np.float32).astype(ml_dtypes.bfloat16)
    onesr = np.ones((1, 128), dtype=np.float32)

    if _NC_CACHE is None:
        _NC_CACHE = build()
    nc = _NC_CACHE
    in_maps = []
    for c in range(N_CORES):
        shard = np.ascontiguousarray(x[2 * c:2 * c + 2].reshape(TPC, D))
        in_maps.append(dict(x=shard, wq=wq, wkv=wkv, wo=wo, wf=wf,
                            gam=gam, bet=bet, mask01=mask01, idf=idf,
                            idb=idb, onesr=onesr))
    res = run_bass_kernel_spmd(nc, in_maps, core_ids=list(range(N_CORES)))
    out = np.concatenate(
        [np.asarray(res.results[c]["y"]).reshape(2, SEQ, D) for c in range(N_CORES)],
        axis=0)
    return out.astype(np.float32)


# version 27
# speedup vs baseline: 1.0101x; 1.0017x over previous
import numpy as np
import ml_dtypes
import concourse.bass as bass
import concourse.mybir as mybir
from concourse.bass_utils import run_bass_kernel_spmd

F32 = mybir.dt.float32
BF16 = mybir.dt.bfloat16
U8 = mybir.dt.uint8
AF = mybir.ActivationFunctionType
ALU = mybir.AluOpType
AX = mybir.AxisListType

N_CORES = 8
SEQ = 2048
D = 1024
H = 16
TPC = 4096
NT = TPC // 128
NTS = SEQ // 128
MAGIC = 12582912.0
EXP_C = -40.0
EPS = 1e-5


class Sem:
    def __init__(self, nc, name):
        self.h = nc.alloc_semaphore(name=name)
        self.n = 0

    def bump(self, k=1):
        self.n += k
        return (self, k)


def build():
    nc = bass.Bass("TRN2", target_bir_lowering=False)

    x_d = nc.dram_tensor("x", [TPC, D], F32, kind="ExternalInput")
    wq_d = nc.dram_tensor("wq", [D, D], F32, kind="ExternalInput")
    wkv_d = nc.dram_tensor("wkv", [D, 2 * D], F32, kind="ExternalInput")
    wo_d = nc.dram_tensor("wo", [D, D], F32, kind="ExternalInput")
    wf_d = nc.dram_tensor("wf", [D, D], F32, kind="ExternalInput")
    gam_d = nc.dram_tensor("gam", [1, D], F32, kind="ExternalInput")
    bet_d = nc.dram_tensor("bet", [1, D], F32, kind="ExternalInput")
    mask_d = nc.dram_tensor("mask01", [128, 128], BF16, kind="ExternalInput")
    idf_d = nc.dram_tensor("idf", [128, 128], F32, kind="ExternalInput")
    idb_d = nc.dram_tensor("idb", [128, 128], BF16, kind="ExternalInput")
    onesr_d = nc.dram_tensor("onesr", [1, 128], F32, kind="ExternalInput")
    y_d = nc.dram_tensor("y", [TPC, D], F32, kind="ExternalOutput")
    x2s_d = nc.dram_tensor("x2s", [TPC, D], F32, kind="Internal")
    xn2s_d = nc.dram_tensor("xn2s", [TPC, D], F32, kind="Internal")
    cc_in_d = nc.dram_tensor("cc_in", [1, 16], F32, kind="Internal")
    cc_out_d = nc.dram_tensor("cc_out", [1, 16], F32, kind="Internal",
                              addr_space="Shared")

    A = nc.alloc_sbuf_tensor
    mask_sb = A("mask_sb", [128, 128], BF16)
    idf_sb = A("idf_sb", [128, 128], F32)
    idb_sb = A("idb_sb", [128, 128], BF16)
    onesr_sb = A("onesr_sb", [1, 128], F32)
    gam_sb = A("gam_sb", [1, D], F32)
    bet_sb = A("bet_sb", [1, D], F32)
    gb_sb = A("gb_sb", [128, D], F32)
    bb_sb = A("bb_sb", [128, D], F32)
    wqt_sb = A("wqt_sb", [128, 8 * D], BF16)
    bn12 = A("bn12", [128, 12], F32)
    bn2 = A("bn2", [128, 2], F32)
    vp = A("vp", [128, 1], F32)
    iv = A("iv", [128, 1], F32)
    rr = A("rr", [128, 1], F32)
    bcol = A("bcol", [128, 1], F32)
    bn12b = A("bn12b", [128, 12], F32)
    bn2b = A("bn2b", [128, 2], F32)
    vpb = A("vpb", [128, 1], F32)
    ivb = A("ivb", [128, 1], F32)
    rrb = A("rrb", [128, 1], F32)
    bcolb = A("bcolb", [128, 1], F32)
    mxcol = A("mxcol", [128, 1], F32)
    tmax = A("tmax", [128, 1], F32)
    tmax16 = A("tmax16", [128, 16], F32)
    SA = [dict(s6=A(f"sa6_{i}", [128, 12], F32), s2=A(f"sa2_{i}", [128, 2], F32),
               vp=A(f"savp_{i}", [128, 1], F32), iv=A(f"saiv_{i}", [128, 1], F32),
               rt=A(f"sar_{i}", [128, 1], F32), bt=A(f"sab_{i}", [128, 1], F32))
          for i in range(2)]
    SB = [dict(s6=A(f"sb6_{i}", [128, 12], F32), s2=A(f"sb2_{i}", [128, 2], F32),
               vp=A(f"sbvp_{i}", [128, 1], F32), iv=A(f"sbiv_{i}", [128, 1], F32),
               rt=A(f"sbr_{i}", [128, 1], F32), bt=A(f"sbb_{i}", [128, 1], F32))
          for i in range(2)]
    rc_sb = A("rc_sb", [128, 7], F32)
    ones_col = A("ones_col", [128, 1], F32)
    ws8 = A("ws8", [128, 8], F32)
    wm8 = A("wm8", [128, 8], F32)
    wc8 = A("wc8", [128, 8], F32)
    dcol = A("dcol", [128, 1], F32)
    ndcol = A("ndcol", [128, 1], F32)
    srow = A("srow", [1, 12], F32)
    gmr = A("gmr", [1, 4], F32)
    alpha = A("alpha", [1, 1], F32)
    invs_col = A("invs_col", [128, 1], F32)
    sa_col = A("sa_col", [128, 1], F32)

    AR, off = {}, 0

    def region(name, size):
        nonlocal off
        AR[name] = off
        off += (size + 31) & ~31

    region("xT", 8192)
    region("xn", 32768)
    region("wv0", 16384)
    region("wv1", 16384)
    region("K", 32768)
    region("V", 33280)
    region("Q", 32768)
    ARENA = off
    arena = A("arena", [128, ARENA], U8)
    abase = nc.lookup_mloc(arena).addr

    def place(name, shape, dtype, reg, extra=0):
        hnd = nc.virtual_sbuf_tensor(name, list(shape), dtype)
        nc._tensor(hnd.name, list(hnd.shape), hnd.dtype, type="SB")
        mloc = nc.lookup_mloc(hnd)
        mloc.allocated = True
        mloc.addr = abase + AR[reg] + extra
        return hnd

    xT = [place(f"xT{i}", [128, 1024], F32, "xT", 4096 * i) for i in range(2)]
    xn_fm = place("xn_fm", [128, 16384], BF16, "xn")
    attn_tm = place("attn_tm", [128, 16384], BF16, "xn")
    wv0 = place("wv0", [128, 8192], BF16, "wv0")
    wo_sb = place("wo_sb", [128, 8192], BF16, "wv0")
    wv1 = place("wv1", [128, 8192], BF16, "wv1")
    E_sb = place("E_sb", [128, 6144], BF16, "wv1")
    ternB = place("ternB", [128, 1024], BF16, "wv1", 12288)
    ternB2 = place("ternB2", [128, 1024], BF16, "K", 28672)
    yO2 = [place("yO", [128, 1024], F32, "wv1", 0),
           place("yO1", [128, 1024], F32, "Q", 0)]
    hO2 = [place("hO", [128, 1024], F32, "wv1", 4096),
           place("hO1", [128, 1024], F32, "Q", 4096)]
    K_fm = place("K_fm", [128, 16384], BF16, "K")
    x2O = [place(f"x2O{i}", [128, 1024], F32, "K", 4096 * i) for i in range(2)]
    wfb = [place(f"wfb{i}", [128, 1024], F32, "K", 8192 + 4096 * i) for i in range(2)]
    g1W = place("g1W", [128, 1024], F32, "K", 16384)
    g2W = place("g2W", [128, 1024], F32, "K", 20480)
    ternW = place("ternW", [128, 1024], F32, "K", 24576)
    t1O2 = [place("t1O", [128, 1024], F32, "K", 28672),
            place("t1O1", [128, 1024], F32, "Q", 8192)]
    V_aug = place("V_aug", [128, 16640], BF16, "V")
    attn_fm = place("attn_fm", [128, 16384], BF16, "V")
    Q_fm = place("Q_fm", [128, 16384], BF16, "Q")
    xnC = [place(f"xnC{i}", [128, 1024], F32, "Q", 4096 * i) for i in range(2)]
    t1C = place("t1C", [128, 1024], F32, "Q", 8192)
    intsT = [place(f"intsT{i}", [128, 1024], BF16, "Q", 12288 + 2048 * i) for i in range(2)]
    intsF = [place(f"intsF{i}", [128, 1024], BF16, "Q", 16384 + 2048 * i) for i in range(2)]
    outC = [place(f"outC{i}", [128, 1024], F32, "Q", 20480 + 4096 * i) for i in range(2)]

    psS = [nc.alloc_psum_tensor(f"psS{i}", [128, 1024], F32) for i in range(2)]
    psPV = [nc.alloc_psum_tensor(f"psPV{i}", [128, 512], F32) for i in range(3)]
    psX = nc.alloc_psum_tensor("psX", [128, 512], F32)

    s_in = Sem(nc, "s_in")
    s_w = Sem(nc, "s_w")
    s_tr = Sem(nc, "s_tr")
    s_mm = Sem(nc, "s_mm")
    s_v = Sem(nc, "s_v")
    s_a = Sem(nc, "s_a")
    s_st = Sem(nc, "s_st")
    s_cc = Sem(nc, "s_cc")
    s_g = Sem(nc, "s_g")
    s_y = Sem(nc, "s_y")

    sched = {k: [] for k in ("sync", "tensor", "vector", "scalar", "gpsimd")}

    def emit(eng, fn, waits=(), incs=()):
        waits = tuple((s.h, int(v)) for (s, v) in waits if v > 0)
        incs = tuple((s.h, int(k)) for (s, k) in incs)

        def run(e, fn=fn, waits=waits, incs=incs):
            for h, v in waits:
                e.wait_ge(h, v)
            inst = fn(e)
            for h, k in incs:
                inst.then_inc(h, k)
        sched[eng].append(run)

    def w(sem):
        return (sem, sem.n)

    def vwait(sem):
        return [(sem, sem.n)]

    tp_state = {"g": 0, "hist": []}

    def transpose_blocks(src_fn, dst_fn, n, first_waits, evac="vector"):
        first = True
        for g0 in range(0, n, 4):
            cnt = min(4, n - g0)
            buf = psX if tp_state["g"] % 2 == 0 else psPV[0]
            tp_state["g"] += 1
            guard = tp_state["hist"][-2] if len(tp_state["hist"]) >= 2 else None
            for j in range(g0, g0 + cnt):
                wl = []
                if j == g0:
                    wl = [guard] if guard else []
                    if first:
                        wl += list(first_waits)
                        first = False
                emit("tensor", lambda e, j=j, g0=g0, buf=buf: e.matmul(
                    buf[:, 128 * (j - g0):128 * (j - g0) + 128],
                    src_fn(j), idb_sb[:, :], start=True, stop=True),
                    waits=wl, incs=([s_mm.bump()] if j == g0 + cnt - 1 else []))
            if evac == "vector":
                emit("vector", lambda e, g0=g0, cnt=cnt, buf=buf: e.tensor_copy(
                    dst_fn(g0, cnt),
                    buf.ap()[:, 0:128 * cnt].rearrange("p (b r) -> p b r", r=128)),
                    waits=vwait(s_mm), incs=[s_v.bump()])
                tp_state["hist"].append((s_v, s_v.n))
            else:
                emit("scalar", lambda e, g0=g0, cnt=cnt, buf=buf: e.activation(
                    dst_fn(g0, cnt),
                    buf.ap()[:, 0:128 * cnt].rearrange("p (b r) -> p b r", r=128),
                    AF.Identity, bias=np.float32(0.0), scale=np.float32(1.0)),
                    waits=vwait(s_mm), incs=[s_a.bump()])
                tp_state["hist"].append((s_a, s_a.n))
        return tp_state["hist"][-1]

    for dst, src in ((mask_sb, mask_d), (idf_sb, idf_d), (idb_sb, idb_d),
                     (onesr_sb, onesr_d), (gam_sb, gam_d), (bet_sb, bet_d)):
        emit("sync", lambda e, d=dst, s=src: e.dma_start(out=d[:, :], in_=s[:, :]),
             incs=[s_in.bump(16)])
    consts_in = s_in.n
    for t, val in ((ones_col, 1.0), (mxcol, 0.0)):
        emit("vector", lambda e, t=t, v=val: e.memset(t[:, :], v))
    emit("vector", lambda e: e.memset(
        V_aug.ap().rearrange("p (t r) -> p t r", t=16)[:, :, 64:1040:65], 1.0))
    emit("vector", lambda e: e.drain(), incs=[s_v.bump()])
    memset_v = s_v.n
    for which, (src, dst) in enumerate(((gam_sb, gb_sb), (bet_sb, bb_sb))):
        for hf in range(2):
            emit("tensor", lambda e, s=src, hf=hf: e.matmul(
                psX[:, 0:512], onesr_sb[:, :], s[0:1, 512 * hf:512 * hf + 512],
                start=True, stop=True),
                waits=[(s_in, consts_in)] + ([] if which == 0 and hf == 0 else vwait(s_v)),
                incs=[s_mm.bump()])
            emit("vector", lambda e, d=dst, hf=hf: e.tensor_copy(
                d[:, 512 * hf:512 * hf + 512], psX[:, 0:512]),
                waits=vwait(s_mm), incs=[s_v.bump()])
    gbeta_v = s_v.n

    wkv3 = wkv_d.ap().rearrange("(c p) i -> p c i", p=128)
    wq3 = wq_d.ap().rearrange("(c p) i -> p c i", p=128)
    wo3 = wo_d.ap().rearrange("(c p) i -> p c i", p=128)
    wf3 = wf_d.ap().rearrange("(c p) i -> p c i", p=128)

    def stats_part1(src_ap, S, extra_waits=(), eps_mul=None):
        s6, s2, vpt, ivt, rt = (S[k] for k in ("s6", "s2", "vp", "iv", "rt"))
        emit("vector", lambda e, a=s6, b=src_ap: e.bn_stats(a[:, 0:6], b[:, 0:512]),
             waits=list(extra_waits))
        emit("vector", lambda e, a=s6, b=src_ap: e.bn_stats(a[:, 6:12], b[:, 512:1024]))
        emit("vector", lambda e: e.drain())
        emit("vector", lambda e, a=s2, b=s6: e.bn_aggr(a[:, :], b[:, :]))
        emit("vector", lambda e: e.drain())
        if eps_mul is None:
            emit("vector", lambda e, a=vpt, b=s2: e.tensor_scalar(
                a[:, :], b[:, 1:2], EPS, None, ALU.add))
        else:
            m, cadd = eps_mul
            emit("vector", lambda e, a=vpt, b=s2, m=m, c=cadd: e.tensor_scalar(
                a[:, :], b[:, 1:2], m, c, ALU.mult, ALU.add))
        emit("vector", lambda e: e.drain())
        emit("vector", lambda e, a=ivt, b=vpt: e.reciprocal(a[:, :], b[:, :]),
             incs=[s_v.bump()])
        emit("scalar", lambda e, a=rt, b=ivt: e.activation(a[:, :], b[:, :], AF.Sqrt),
             waits=vwait(s_v), incs=[s_a.bump()])
        return s_a.n

    def stats_part2(S, sqrt_a):
        s2, rt, bt = S["s2"], S["rt"], S["bt"]
        emit("vector", lambda e, a=bt, b=s2, c=rt: e.scalar_tensor_tensor(
            a[:, :], b[:, 0:1], -1.0, c[:, :], ALU.mult, ALU.mult),
            waits=[(s_a, sqrt_a)])
        emit("vector", lambda e: e.drain(), incs=[s_v.bump()])

    def stats_block2(src_ap, S, extra_waits=()):
        sa = stats_part1(src_ap, S, extra_waits)
        stats_part2(S, sa)

    def stats_block(src_ap, s6, s2, vpt, ivt, rt, bt, extra_waits=()):
        stats_block2(src_ap, dict(s6=s6, s2=s2, vp=vpt, iv=ivt, rt=rt, bt=bt),
                     extra_waits)

    o_evac_last = [None]
    for sq in range(2):
        tbase = NTS * sq

        vec_now = s_v.n
        emit("sync", lambda e, t=tbase: e.dma_start(
            out=xT[0][:, :], in_=x_d[128 * t:128 * (t + 1), :]),
            waits=[(s_v, vec_now)], incs=[s_in.bump(16)])
        emit("gpsimd", lambda e: e.dma_start(
            out=wv0.ap().rearrange("p (c i) -> p c i", c=8),
            in_=wkv3[:, :, 0:1024]),
            waits=[(s_mm, s_mm.n)], incs=[s_w.bump(16)])
        wk_ready = s_w.n

        evac_hist = []
        base_v = s_v.n
        base_st = s_st.n
        base_a = s_a.n
        slot_ctr = [0]

        def emit_proj_group(dst, w_sb, w_ready_val, tb, kc, extra):
            slot = slot_ctr[0]
            slot_ctr[0] += 1
            p = psS[slot % 2][:, 512 * ((slot // 2) % 2):
                              512 * ((slot // 2) % 2) + 512]
            guard = evac_hist[-4] if len(evac_hist) >= 4 else base_v
            for c in range(8):
                wl = []
                if c == 0:
                    wl = [(s_w, w_ready_val), (s_v, guard)] + extra
                emit("tensor", lambda e, kc=kc, tb=tb, c=c, p=p, ws=w_sb: e.matmul(
                    p, ws[:, 1024 * c + 128 * kc:1024 * c + 128 * kc + 128],
                    xn_fm[:, 2048 * c + 512 * tb:2048 * c + 512 * tb + 512],
                    start=(c == 0), stop=(c == 7)),
                    waits=wl, incs=([s_mm.bump()] if c == 7 else []))
            emit("vector", lambda e, kc=kc, tb=tb, p=p, d=dst: e.tensor_copy(
                d[:, 2048 * kc + 512 * tb:2048 * kc + 512 * tb + 512], p),
                waits=vwait(s_mm)
                + ([(s_st, base_st), (s_a, base_a)]
                   if len(evac_hist) == 0 else []),
                incs=[s_v.bump()])
            evac_hist.append(s_v.n)

        kq = [(tb, kc) for tb in range(4) for kc in range(8)]
        kq_i = [0]
        tr_ev_tile = {}
        xn_act_hist = {}
        tr_hist = {}
        for tt in range(NTS):
            buf = tt % 2
            xt_wait = s_in.n
            if tt + 1 < NTS:
                g = xn_act_hist.get((tt + 1) % 2, 0)
                emit("sync", lambda e, t=tbase + tt + 1, b=(tt + 1) % 2: e.dma_start(
                    out=xT[b][:, :], in_=x_d[128 * t:128 * (t + 1), :]),
                    waits=[(s_a, g), (s_v, vec_now)], incs=[s_in.bump(16)])
            stats_block(xT[buf], bn12, bn2, vp, iv, rr, bcol,
                        extra_waits=[(s_in, xt_wait)]
                        + ([(s_v, gbeta_v)] if tt == 0 and sq == 0 else []))
            emit("scalar", lambda e, b=buf: e.activation(
                E_sb[:, 2048 * b:2048 * b + 1024], xT[b][:, :], AF.Identity,
                bias=bcol[:, :], scale=rr[:, :]),
                waits=vwait(s_v) + ([(s_mm, tr_hist[buf])] if buf in tr_hist else [])
                + ([(s_g, s_g.n)] if tt == 0 else []),
                incs=[s_a.bump()])
            xn_act_hist[buf] = s_a.n
            tr_ev_tile[tt] = transpose_blocks(
                lambda j, b=buf: E_sb[:, 2048 * b + 128 * j:2048 * b + 128 * j + 128],
                lambda g0, cnt, t=tt: xn_fm.ap().rearrange(
                    "p (c r) -> p c r", c=8)[:, g0:g0 + cnt,
                                             128 * t:128 * t + 128],
                8, [(s_a, xn_act_hist[buf])])
            tr_hist[buf] = s_mm.n
            ready = 8 * ((tt - 3) // 4 + 1) if tt >= 3 else 0
            budget = 3 if sq == 1 else 0
            while budget > 0 and kq_i[0] < ready:
                tb, kc = kq[kq_i[0]]
                emit_proj_group(K_fm, wv0, wk_ready, tb, kc,
                                [tr_ev_tile[4 * tb + 3]] if kc == 0 else [])
                kq_i[0] += 1
                budget -= 1
        while kq_i[0] < len(kq):
            tb, kc = kq[kq_i[0]]
            emit_proj_group(K_fm, wv0, wk_ready, tb, kc,
                            [tr_ev_tile[4 * tb + 3]] if kc == 0 else [])
            kq_i[0] += 1
        xn_tr_v = s_v.n

        emit("gpsimd", lambda e: e.dma_start(
            out=wv1.ap().rearrange("p (c i) -> p c i", c=8),
            in_=wkv3[:, :, 1024:2048]),
            waits=[tr_ev_tile[NTS - 1]], incs=[s_w.bump(16)])
        wv_ready = s_w.n

        emit("gpsimd", lambda e: e.dma_start(
            out=wv0.ap().rearrange("p (c i) -> p c i", c=8), in_=wq3),
            waits=[(s_mm, s_mm.n)], incs=[s_w.bump(16)])
        wq_ready = s_w.n

        emit("vector", lambda e: e.memset(
            V_aug.ap().rearrange("p (t r) -> p t r", t=16)[:, :, 64:1040:65], 1.0),
            waits=[(s_mm, s_mm.n)] + ([(s_v, memset_v)] if sq == 0 else [])
            + ([o_evac_last[0]] if o_evac_last[0] else []))
        emit("vector", lambda e: e.drain(), incs=[s_v.bump()])
        ones_v = s_v.n
        for tt in range(NTS):
            p = psS[tt % 2]
            guard = evac_hist[-2] if len(evac_hist) >= 2 else base_v
            for c in range(8):
                for hf in range(2):
                    wl = []
                    if c == 0 and hf == 0:
                        wl = [(s_w, wv_ready), (s_v, guard)]
                    emit("tensor", lambda e, t=tt, c=c, hf=hf, p=p: e.matmul(
                        p[:, 512 * hf:512 * hf + 512],
                        xn_fm[:, 2048 * c + 128 * t:2048 * c + 128 * t + 128],
                        wv1[:, 1024 * c + 512 * hf:1024 * c + 512 * hf + 512],
                        start=(c == 0), stop=(c == 7)),
                        waits=wl,
                        incs=([s_mm.bump()] if c == 7 and hf == 1 else []))
            emit("vector", lambda e, t=tt, p=p: e.tensor_copy(
                V_aug[:, 1040 * t:1040 * t + 1040]
                     .rearrange("p (h r) -> p h r", r=65)[:, :, 0:64],
                p.ap().rearrange("p (h r) -> p h r", h=16)),
                waits=vwait(s_mm) + ([(s_v, ones_v)] if tt == 0 else []),
                incs=[s_v.bump()])
            evac_hist.append(s_v.n)

        for tb in range(4):
            for kc in range(8):
                emit_proj_group(Q_fm, wv0, wq_ready, tb, kc, [])
        q_done_v = s_v.n
        q_done_mm = s_mm.n

        emit("gpsimd", lambda e: e.dma_start(
            out=wo_sb.ap().rearrange("p (c i) -> p c i", c=8), in_=wo3),
            waits=[(s_mm, q_done_mm)], incs=[s_w.bump(16)])
        wo_ready = s_w.n

        steps = []
        for h in range(H):
            for pas in range(2):
                for jj in (range(8) if pas == 0 else range(16)):
                    steps.append((h, pas, jj))
        exp_cnt = []
        mm_step = []
        pv_ev = s_v.n
        head_started_banks = {}
        head_first_pv = {}

        def emit_S(k):
            h, pas, jj = steps[k]
            hc, po = h // 2, 64 * (h % 2)
            ps = psS[k % 2]
            qlo = max(128 * jj, 1024 * pas)
            qhi = 1024 * (pas + 1)
            q0 = qlo
            first = True
            while q0 < qhi:
                q1 = min(q0 + 512, qhi,
                         1024 * pas + ((q0 - 1024 * pas) // 512 + 1) * 512)
                wl = []
                if first:
                    if k >= 2:
                        wl.append((s_a, exp_cnt[k - 2]))
                    else:
                        wl.append((s_v, q_done_v))
                    first = False
                emit("tensor", lambda e, q0=q0, q1=q1, p=ps, hc=hc,
                     po=po, jj=jj, pas=pas: e.matmul(
                         p[:, q0 - 1024 * pas:q1 - 1024 * pas],
                         K_fm[po:po + 64,
                              2048 * hc + 128 * jj:2048 * hc + 128 * jj + 128],
                         Q_fm[po:po + 64, 2048 * hc + q0:2048 * hc + q1],
                         start=True, stop=True),
                     waits=wl,
                     incs=([s_mm.bump()] if q1 == qhi else []))
                q0 = q1
            mm_step.append(s_mm.n)

        msk_of = {}

        def emit_exp_mask(k):
            h, pas, jj = steps[k]
            ps = psS[k % 2]
            eoff = 2048 * (k % 3)
            qlo = max(128 * jj, 1024 * pas)
            qhi = 1024 * (pas + 1)
            emit("scalar", lambda e, qlo=qlo, qhi=qhi, p=ps, eoff=eoff,
                 pas=pas: e.activation(
                     E_sb[:, eoff + qlo - 1024 * pas:eoff + qhi - 1024 * pas],
                     p[:, qlo - 1024 * pas:qhi - 1024 * pas],
                     AF.Exp, bias=np.float32(EXP_C), scale=np.float32(0.125)),
                 waits=[(s_mm, mm_step[k])], incs=[s_a.bump()])
            exp_cnt.append(s_a.n)
            if 1024 * pas <= 128 * jj < 1024 * (pas + 1):
                emit("vector", lambda e, jj=jj, eoff=eoff, pas=pas: e.tensor_tensor(
                    E_sb[:, eoff + 128 * jj - 1024 * pas:
                         eoff + 128 * jj - 1024 * pas + 128],
                    E_sb[:, eoff + 128 * jj - 1024 * pas:
                         eoff + 128 * jj - 1024 * pas + 128],
                    mask_sb[:, :], ALU.mult),
                    waits=[(s_a, exp_cnt[k])], incs=[s_v.bump()])
                msk_of[k] = s_v.n

        def emit_PV(k):
            nonlocal pv_ev
            h, pas, jj = steps[k]
            eoff = 2048 * (k % 3)
            ilo = max(jj, 8 * pas)
            ihi = 8 * (pas + 1)
            order = [i for i in range(ilo, ihi) if i != jj]
            if ilo <= jj < ihi:
                order.append(jj)
            started = head_started_banks.setdefault(h, set())
            for n_i, i in enumerate(order):
                bnk, sl = i // 7, i % 7
                wl = []
                if n_i == 0:
                    wl.append((s_a, exp_cnt[k]))
                    if not head_first_pv.get(h):
                        wl.append((s_v, pv_ev))
                        head_first_pv[h] = True
                if k in msk_of and i == jj:
                    wl.append((s_v, msk_of[k]))
                last = (jj == i)
                fin = last and (i in (6, 13, 15))
                st_flag = bnk not in started
                if st_flag:
                    started.add(bnk)
                emit("tensor", lambda e, i=i, jj=jj, h=h, eoff=eoff,
                     bnk=bnk, sl=sl, pas=pas, last=last, st=st_flag: e.matmul(
                         psPV[bnk][:, 65 * sl:65 * sl + 65],
                         E_sb[:, eoff + 128 * i - 1024 * pas:
                              eoff + 128 * i - 1024 * pas + 128],
                         V_aug[:, 1040 * jj + 65 * h:1040 * jj + 65 * h + 65],
                         start=st, stop=last),
                     waits=wl,
                     incs=([s_mm.bump()] if fin else []))
            if pas == 1 and jj == 15:
                base_mm = s_mm.n - 3
                for bnk, ntile in ((0, 7), (1, 7), (2, 2)):
                    emit("vector", lambda e, bnk=bnk, nt=ntile: e.reciprocal(
                        rc_sb[:, 0:nt], psPV[bnk][:, 64:64 + (nt - 1) * 65 + 1:65]),
                        waits=[(s_mm, base_mm + bnk + 1)])
                    emit("vector", lambda e: e.drain())
                    emit("vector", lambda e, bnk=bnk, nt=ntile, h=h: e.tensor_tensor(
                        attn_tm.ap().rearrange("p (t r) -> p t r", r=1024)
                               [:, 7 * bnk:7 * bnk + nt, 64 * h:64 * h + 64],
                        psPV[bnk][:, 0:nt * 65].rearrange("p (t r) -> p t r", r=65)
                                [:, :, 0:64],
                        rc_sb[:, 0:nt].unsqueeze(2).broadcast_to([128, nt, 64]),
                        ALU.mult))
                    emit("vector", lambda e: e.drain(), incs=[s_v.bump()])
                pv_ev = s_v.n

        emit_S(0)
        for k in range(len(steps)):
            emit_exp_mask(k)
            if k + 1 < len(steps):
                emit_S(k + 1)
            if k >= 1:
                emit_PV(k - 1)
        emit_PV(len(steps) - 1)
        attn_v = s_v.n
        attn_mm = s_mm.n
        exp_last = exp_cnt[-1]

        vec_now = s_v.n
        emit("sync", lambda e, t=tbase: e.dma_start(
            out=xT[0][:, :], in_=x_d[128 * t:128 * (t + 1), :]),
            waits=[(s_a, xn_act_hist[0])], incs=[s_in.bump(16)])
        x2v_hist, apply_v, st_hist, gh_hist = {}, {}, {}, {}
        h_v, mx_v_of, pend = {}, {}, {}
        mx_v_last = [0]
        tr_v_of = {}

        def o_transposes(t2, first):
            tr_v_of[t2] = transpose_blocks(
                lambda j, t=t2: attn_tm[:, 1024 * t + 128 * j:1024 * t + 128 * j + 128],
                lambda g0, cnt, t=t2: attn_fm.ap().rearrange(
                    "p (c r) -> p c r", c=8)[:, g0:g0 + cnt,
                                             128 * t:128 * t + 128],
                8, [(s_v, attn_v)] if first else [], evac="scalar")

        def emit_redmax(pt):
            b = pt % 2
            emit("vector", lambda e, b=b, pt=pt: e.reduce_max(
                tmax16[:, pt:pt + 1], t1O2[b][:, :], axis=AX.X,
                apply_absolute_value=True),
                waits=[(s_g, gh_hist[(pt, "g")])], incs=[s_v.bump()])
            mx_v_of[b] = s_v.n
            mx_v_last[0] = s_v.n

        def apply_unit(pt):
            b = pt % 2
            Sa = SA[b]
            sa_a = pend[b]
            stats_part2(Sa, sa_a)
            emit("vector", lambda e, b=b, Sa=Sa: e.tensor_scalar(
                hO2[b][:, :], x2O[b][:, :], Sa["rt"][:, :], Sa["bt"][:, :],
                ALU.mult, ALU.add),
                waits=[(s_g, gh_hist.get(b, 0))], incs=[s_v.bump()])
            h_v[b] = apply_v[b] = s_v.n
            emit("gpsimd", lambda e, b=b: e.tensor_mul(
                t1O2[b][:, :], hO2[b][:, :], gb_sb[:, :]),
                waits=[(s_v, h_v[b]), (s_st, st_hist.get(("xn2", b), 0)),
                       (s_v, mx_v_of.get(b, 0))])
            emit("gpsimd", lambda e, b=b: e.tensor_add(
                t1O2[b][:, :], t1O2[b][:, :], bb_sb[:, :]), incs=[s_g.bump()])
            gh_hist[b] = gh_hist[(pt, "g")] = s_g.n
            emit("sync", lambda e, t=tbase + pt, b=b: e.dma_start(
                out=xn2s_d[128 * t:128 * (t + 1), :], in_=t1O2[b][:, :]),
                waits=[(s_g, gh_hist[b])], incs=[s_st.bump(16)])
            st_hist[("xn2", b)] = s_st.n
            if pt > 0:
                emit_redmax(pt - 1)

        o_transposes(0, True)
        for tt in range(NTS):
            buf = tt % 2
            xt_wait = s_in.n
            if tt + 1 < NTS:
                g = x2v_hist.get((tt + 1) % 2, 0)
                ga = xn_act_hist[(tt + 1) % 2] if tt == 0 else 0
                emit("sync", lambda e, t=tbase + tt + 1, b=(tt + 1) % 2,
                     g=g, ga=ga: e.dma_start(
                         out=xT[b][:, :], in_=x_d[128 * t:128 * (t + 1), :]),
                     waits=[(s_v, g), (s_a, ga)], incs=[s_in.bump(16)])
                o_transposes(tt + 1, False)
            p = psS[tt % 2]
            for c in range(8):
                for hf in range(2):
                    wl = []
                    if c == 0 and hf == 0:
                        wl = [tr_v_of[tt], (s_w, wo_ready)]
                        if tt < 2:
                            wl.append((s_a, exp_last))
                        else:
                            wl.append((s_v, x2v_hist[tt % 2]))
                    emit("tensor", lambda e, t=tt, c=c, hf=hf, p=p: e.matmul(
                        p[:, 512 * hf:512 * hf + 512],
                        attn_fm[:, 2048 * c + 128 * t:2048 * c + 128 * t + 128],
                        wo_sb[:, 1024 * c + 512 * hf:1024 * c + 512 * hf + 512],
                        start=(c == 0), stop=(c == 7)),
                        waits=wl,
                        incs=([s_mm.bump()] if c == 7 and hf == 1 else []))
            emit("vector", lambda e, b=buf, p=p: e.tensor_add(
                x2O[b][:, :], p[:, :], xT[b][:, :]),
                waits=vwait(s_mm) + [(s_in, xt_wait),
                                     (s_st, st_hist.get(buf, 0))])
            emit("vector", lambda e: e.drain(), incs=[s_v.bump()])
            x2v_hist[buf] = s_v.n
            emit("sync", lambda e, t=tbase + tt, b=buf: e.dma_start(
                out=x2s_d[128 * t:128 * (t + 1), :], in_=x2O[b][:, :]),
                waits=[(s_v, x2v_hist[buf])], incs=[s_st.bump(16)])
            st_hist[buf] = s_st.n
            pend[buf] = stats_part1(x2O[buf], SA[buf],
                                    eps_mul=(1.0 + EPS, EPS * EPS))
            if tt > 0:
                apply_unit(tt - 1)
        apply_unit(NTS - 1)
        emit_redmax(NTS - 1)
        emit("vector", lambda e: e.drain())
        emit("vector", lambda e: e.reduce_max(tmax[:, :], tmax16[:, :], axis=AX.X))
        emit("vector", lambda e: e.drain())
        emit("vector", lambda e: e.tensor_max(mxcol[:, :], mxcol[:, :], tmax[:, :]))
        emit("vector", lambda e: e.drain(), incs=[s_v.bump()])
        mx_v_last[0] = s_v.n
        o_evac_last[0] = tr_v_of[NTS - 1]

        if sq == 0:
            emit("gpsimd", lambda e: e.dma_start(out=wfb[0][:, :], in_=wf3[:, 0, :]),
                 waits=[(s_v, attn_v)], incs=[s_w.bump(16)])
            wf_rd = {}
            for c in range(8):
                wf_wait = s_w.n
                if c + 1 < 8:
                    emit("gpsimd", lambda e, c=c: e.dma_start(
                        out=wfb[(c + 1) % 2][:, :], in_=wf3[:, c + 1, :]),
                        waits=[(s_v, wf_rd.get((c + 1) % 2, 0))],
                        incs=[s_w.bump(16)])
                emit("vector", lambda e, c=c: e.reduce_sum(
                    ws8[:, c:c + 1], wfb[c % 2][:, :], axis=AX.X,
                    apply_absolute_value=True),
                    waits=[(s_w, wf_wait)], incs=[s_v.bump()])
                wf_rd[c % 2] = s_v.n
            emit("tensor", lambda e: e.matmul(
                psX[0:1, 0:8], ones_col[:, :], ws8[:, :], start=True, stop=True),
                waits=vwait(s_v), incs=[s_mm.bump()])
            emit("vector", lambda e: e.reduce_sum(srow[0:1, 0:1], psX[0:1, 0:8],
                                                  axis=AX.X),
                 waits=vwait(s_mm))
            emit("vector", lambda e: e.drain())
            emit("vector", lambda e: e.tensor_scalar(
                srow[0:1, 1:2], srow[0:1, 0:1], 0.7 / (1024.0 * 1024.0), None, ALU.mult))
            emit("vector", lambda e: e.drain(), incs=[s_v.bump()])
            emit("tensor", lambda e: e.matmul(
                psX[:, 128:129], onesr_sb[:, :], srow[0:1, 1:2], start=True, stop=True),
                waits=vwait(s_v), incs=[s_mm.bump()])
            emit("vector", lambda e: e.tensor_copy(dcol[:, :], psX[:, 128:129]),
                 waits=vwait(s_mm))
            emit("vector", lambda e: e.drain())
            emit("vector", lambda e: e.tensor_scalar(ndcol[:, :], dcol[:, :], -1.0,
                                                     None, ALU.mult))
            emit("vector", lambda e: e.drain(), incs=[s_v.bump()])
            dcol_v = s_v.n
            emit("gpsimd", lambda e: e.dma_start(out=wfb[0][:, :], in_=wf3[:, 0, :]),
                 waits=[(s_v, dcol_v)], incs=[s_w.bump(16)])
            tr_mm_of = {}
            tern_a_of = {}
            wf_rd = {}
            tB = [ternB, ternB2]
            st_xn2_0 = st_hist.get(("xn2", 0), 0)
            for c in range(8):
                wf_wait = s_w.n
                if c + 1 < 8:
                    emit("gpsimd", lambda e, c=c: e.dma_start(
                        out=wfb[(c + 1) % 2][:, :], in_=wf3[:, c + 1, :]),
                        waits=[(s_v, wf_rd.get((c + 1) % 2, 0))],
                        incs=[s_w.bump(16)])
                b = c % 2
                emit("vector", lambda e, b=b: e.tensor_scalar(
                    g1W[:, :], wfb[b][:, :], dcol[:, :], None, ALU.is_gt),
                    waits=[(s_w, wf_wait)])
                emit("vector", lambda e, b=b: e.tensor_scalar(
                    g2W[:, :], wfb[b][:, :], ndcol[:, :], None, ALU.is_lt))
                emit("vector", lambda e: e.drain())
                emit("vector", lambda e: e.tensor_sub(ternW[:, :], g1W[:, :], g2W[:, :]),
                     waits=([(s_a, tern_a_of[c - 1])] if c > 0 else []))
                emit("vector", lambda e: e.drain(), incs=[s_v.bump()])
                tern_w_v = s_v.n
                emit("vector", lambda e, c=c: e.reduce_sum(
                    wc8[:, c:c + 1], ternW[:, :], axis=AX.X,
                    apply_absolute_value=True))
                emit("vector", lambda e, b=b: e.tensor_mul(g1W[:, :], wfb[b][:, :],
                                                           ternW[:, :]))
                emit("vector", lambda e: e.drain())
                emit("vector", lambda e, c=c: e.reduce_sum(
                    wm8[:, c:c + 1], g1W[:, :], axis=AX.X), incs=[s_v.bump()])
                wf_rd[b] = s_v.n
                emit("scalar", lambda e, b=b: e.activation(
                    tB[b][:, :], ternW[:, :], AF.Identity,
                    bias=np.float32(0.0), scale=np.float32(1.0)),
                    waits=[(s_v, tern_w_v)]
                    + ([(s_mm, tr_mm_of[c - 2])] if c > 1 else [])
                    + ([(s_st, st_xn2_0)] if c == 1 else []),
                    incs=[s_a.bump()])
                tern_a_of[c] = s_a.n
                transpose_blocks(
                    lambda ic, b=b: tB[b][:, 128 * ic:128 * ic + 128],
                    lambda g0, cnt, c=c: wqt_sb.ap().rearrange(
                        "p (ic r) -> p ic r", ic=8)[:, g0:g0 + cnt,
                                                    128 * c:128 * c + 128],
                    8, [(s_a, tern_a_of[c])], evac="scalar")
                tr_mm_of[c] = s_mm.n
            emit("tensor", lambda e: e.matmul(
                psX[0:1, 129:137], ones_col[:, :], wm8[:, :], start=True, stop=True),
                waits=vwait(s_v), incs=[s_mm.bump()])
            emit("tensor", lambda e: e.matmul(
                psX[0:1, 137:145], ones_col[:, :], wc8[:, :], start=False, stop=True),
                incs=[s_mm.bump()])
            emit("vector", lambda e: e.reduce_sum(
                srow[0:1, 8:9], psX[0:1, 129:137], axis=AX.X),
                waits=vwait(s_mm))
            emit("vector", lambda e: e.reduce_sum(
                srow[0:1, 9:10], psX[0:1, 137:145], axis=AX.X))
            emit("vector", lambda e: e.drain())
            emit("vector", lambda e: e.tensor_scalar(
                srow[0:1, 2:3], srow[0:1, 9:10], 1.0, None, ALU.max))
            emit("vector", lambda e: e.drain())
            emit("vector", lambda e: e.reciprocal(srow[0:1, 3:4], srow[0:1, 2:3]))
            emit("vector", lambda e: e.drain())
            emit("vector", lambda e: e.tensor_mul(alpha[0:1, :], srow[0:1, 8:9],
                                                  srow[0:1, 3:4]))
            emit("vector", lambda e: e.drain(), incs=[s_v.bump()])

    emit("tensor", lambda e: e.matmul(
        psX[0:1, 0:128], mxcol[:, :], idf_sb[:, :], start=True, stop=True),
        waits=vwait(s_v), incs=[s_mm.bump()])
    emit("vector", lambda e: e.reduce_max(gmr[0:1, 0:1], psX[0:1, 0:128], axis=AX.X),
         waits=vwait(s_mm))
    emit("vector", lambda e: e.drain(), incs=[s_v.bump()])
    emit("sync", lambda e: e.dma_start(out=cc_in_d[0:1, 0:1], in_=gmr[0:1, 0:1]),
         waits=vwait(s_v), incs=[s_st.bump(16)])
    emit("gpsimd", lambda e: e.collective_compute(
        "AllReduce", ALU.max, replica_groups=[list(range(N_CORES))],
        ins=[cc_in_d.ap().opt()], outs=[cc_out_d.ap().opt()]),
        waits=[w(s_st)], incs=[s_cc.bump(1)])
    emit("sync", lambda e: e.dma_start(out=gmr[0:1, 1:2], in_=cc_out_d[0:1, 0:1]),
         waits=[w(s_cc)], incs=[s_in.bump(16)])
    emit("vector", lambda e: e.tensor_scalar(gmr[0:1, 2:3], gmr[0:1, 1:2], 1e-8,
                                             None, ALU.max),
         waits=[w(s_in)])
    emit("vector", lambda e: e.drain())
    emit("vector", lambda e: e.reciprocal(gmr[0:1, 3:4], gmr[0:1, 2:3]))
    emit("vector", lambda e: e.drain())
    emit("vector", lambda e: e.tensor_scalar(srow[0:1, 4:5], gmr[0:1, 3:4], 127.0,
                                             None, ALU.mult))
    emit("vector", lambda e: e.tensor_scalar(srow[0:1, 5:6], gmr[0:1, 2:3], 1.0 / 127.0,
                                             None, ALU.mult))
    emit("vector", lambda e: e.drain())
    emit("vector", lambda e: e.tensor_mul(srow[0:1, 6:7], srow[0:1, 5:6], alpha[0:1, :]))
    emit("vector", lambda e: e.drain(), incs=[s_v.bump()])
    emit("tensor", lambda e: e.matmul(
        psX[:, 200:201], onesr_sb[:, :], srow[0:1, 4:5], start=True, stop=True),
        waits=vwait(s_v), incs=[s_mm.bump()])
    emit("tensor", lambda e: e.matmul(
        psX[:, 201:202], onesr_sb[:, :], srow[0:1, 6:7], start=False, stop=True),
        incs=[s_mm.bump()])
    emit("vector", lambda e: e.tensor_copy(invs_col[:, :], psX[:, 200:201]),
         waits=vwait(s_mm))
    emit("vector", lambda e: e.tensor_copy(sa_col[:, :], psX[:, 201:202]))
    emit("vector", lambda e: e.drain(), incs=[s_v.bump()])
    scale_v = s_v.n
    x2w_done = s_st.n

    emit("sync", lambda e: e.dma_start(out=xnC[0][:, :], in_=xn2s_d[0:128, :]),
         waits=[(s_st, x2w_done), (s_mm, s_mm.n)], incs=[s_in.bump(16)])
    emit("sync", lambda e: e.dma_start(out=xT[0][:, :], in_=x2s_d[0:128, :]),
         incs=[s_in.bump(16)])
    ints_hist, out_hist, y_hist = {}, {}, {}
    trC_hist, mmC_hist = {}, {}

    def c_quant_tr(t, in_wait):
        b = t % 2
        emit("vector", lambda e, b=b: e.tensor_scalar(
            t1C[:, :], xnC[b][:, :], invs_col[:, :], MAGIC, ALU.mult, ALU.add),
            waits=[(s_in, in_wait), (s_v, scale_v)]
            + ([(s_mm, trC_hist[b][1])] if b in trC_hist else []))
        emit("vector", lambda e: e.drain())
        emit("vector", lambda e, b=b: e.tensor_scalar(
            intsT[b][:, :], t1C[:, :], -MAGIC, None, ALU.add))
        emit("vector", lambda e: e.drain(), incs=[s_v.bump()])
        ints_hist[b] = s_v.n
        trC_hist[b] = (transpose_blocks(
            lambda j, b=b: intsT[b][:, 128 * j:128 * j + 128],
            lambda g0, cnt, b=b: intsF[b][:, 128 * g0:128 * (g0 + cnt)]
                .rearrange("p (b2 r) -> p b2 r", r=128),
            8, [(s_v, ints_hist[b])], evac="scalar"), s_mm.n)

    c_quant_tr(0, s_in.n)
    for tt in range(NT):
        buf = tt % 2
        in_next = s_in.n
        if tt + 1 < NT:
            b2 = (tt + 1) % 2
            emit("sync", lambda e, t=tt + 1, b=b2, g=ints_hist.get(b2, 0): e.dma_start(
                out=xnC[b][:, :], in_=xn2s_d[128 * t:128 * (t + 1), :]),
                waits=[(s_v, g)], incs=[s_in.bump(16)])
            emit("sync", lambda e, t=tt + 1, b=b2, g=out_hist.get(b2, 0): e.dma_start(
                out=xT[b][:, :], in_=x2s_d[128 * t:128 * (t + 1), :]),
                waits=[(s_v, g)], incs=[s_in.bump(16)])
            in_next = s_in.n
        p = psS[tt % 2]
        for ic in range(8):
            for hf in range(2):
                wl = []
                if ic == 0 and hf == 0:
                    wl = [trC_hist[buf][0], (s_v, out_hist.get(buf, 0))]
                emit("tensor", lambda e, b=buf, ic=ic, hf=hf, p=p: e.matmul(
                    p[:, 512 * hf:512 * hf + 512],
                    intsF[b][:, 128 * ic:128 * ic + 128],
                    wqt_sb[:, 1024 * ic + 512 * hf:1024 * ic + 512 * hf + 512],
                    start=(ic == 0), stop=(ic == 7)),
                    waits=wl,
                    incs=([s_mm.bump()] if ic == 7 and hf == 1 else []))
        mmC_hist[buf] = s_mm.n
        if tt + 1 < NT:
            c_quant_tr(tt + 1, in_next)
        emit("vector", lambda e, b=buf, p=p: e.scalar_tensor_tensor(
            outC[b][:, :], p[:, :], sa_col[:, :], xT[b][:, :], ALU.mult, ALU.add),
            waits=[(s_mm, mmC_hist[buf]), (s_y, y_hist.get(buf, 0))])
        emit("vector", lambda e: e.drain(), incs=[s_v.bump()])
        out_hist[buf] = s_v.n
        emit("sync", lambda e, t=tt, b=buf: e.dma_start(
            out=y_d[128 * t:128 * (t + 1), :], in_=outC[b][:, :]),
            waits=[(s_v, out_hist[buf])], incs=[s_y.bump(16)])
        y_hist[buf] = s_y.n
    emit("sync", lambda e: e.nop(), waits=[w(s_y)])

    with nc.Block() as blk:
        blk.sync(lambda e: [f(e) for f in sched["sync"]])
        blk.tensor(lambda e: [f(e) for f in sched["tensor"]])
        blk.vector(lambda e: [f(e) for f in sched["vector"]])
        blk.scalar(lambda e: [f(e) for f in sched["scalar"]])
        blk.gpsimd(lambda e: [f(e) for f in sched["gpsimd"]])
    return nc


_NC_CACHE = None


def kernel(L0_attn_x, L0_attn_Wq, L0_attn_Wkv, L0_attn_Wo,
           L0_ffn_weight, L0_ffn_gamma, L0_ffn_beta):
    global _NC_CACHE
    x = np.ascontiguousarray(np.asarray(L0_attn_x, dtype=np.float32))
    wq = np.ascontiguousarray(np.asarray(L0_attn_Wq, dtype=np.float32))
    wkv = np.ascontiguousarray(np.asarray(L0_attn_Wkv, dtype=np.float32))
    wo = np.ascontiguousarray(np.asarray(L0_attn_Wo, dtype=np.float32))
    wf = np.ascontiguousarray(np.asarray(L0_ffn_weight, dtype=np.float32))
    gam = np.ascontiguousarray(np.asarray(L0_ffn_gamma, dtype=np.float32)).reshape(1, D)
    bet = np.ascontiguousarray(np.asarray(L0_ffn_beta, dtype=np.float32)).reshape(1, D)
    mask01 = np.triu(np.ones((128, 128), dtype=np.float32)).astype(ml_dtypes.bfloat16)
    idf = np.eye(128, dtype=np.float32)
    idb = np.eye(128, dtype=np.float32).astype(ml_dtypes.bfloat16)
    onesr = np.ones((1, 128), dtype=np.float32)

    if _NC_CACHE is None:
        _NC_CACHE = build()
    nc = _NC_CACHE
    in_maps = []
    for c in range(N_CORES):
        shard = np.ascontiguousarray(x[2 * c:2 * c + 2].reshape(TPC, D))
        in_maps.append(dict(x=shard, wq=wq, wkv=wkv, wo=wo, wf=wf,
                            gam=gam, bet=bet, mask01=mask01, idf=idf,
                            idb=idb, onesr=onesr))
    res = run_bass_kernel_spmd(nc, in_maps, core_ids=list(range(N_CORES)))
    out = np.concatenate(
        [np.asarray(res.results[c]["y"]).reshape(2, SEQ, D) for c in range(N_CORES)],
        axis=0)
    return out.astype(np.float32)


# version 29 (final)
# speedup vs baseline: 1.0142x; 1.0041x over previous
import numpy as np
import ml_dtypes
import concourse.bass as bass
import concourse.mybir as mybir
from concourse.bass_utils import run_bass_kernel_spmd

F32 = mybir.dt.float32
BF16 = mybir.dt.bfloat16
U8 = mybir.dt.uint8
AF = mybir.ActivationFunctionType
ALU = mybir.AluOpType
AX = mybir.AxisListType

N_CORES = 8
SEQ = 2048
D = 1024
H = 16
TPC = 4096
NT = TPC // 128
NTS = SEQ // 128
MAGIC = 12582912.0
EXP_C = -40.0
EPS = 1e-5


class Sem:
    def __init__(self, nc, name):
        self.h = nc.alloc_semaphore(name=name)
        self.n = 0

    def bump(self, k=1):
        self.n += k
        return (self, k)


def build():
    nc = bass.Bass("TRN2", target_bir_lowering=False)

    x_d = nc.dram_tensor("x", [TPC, D], F32, kind="ExternalInput")
    wq_d = nc.dram_tensor("wq", [D, D], F32, kind="ExternalInput")
    wkv_d = nc.dram_tensor("wkv", [D, 2 * D], F32, kind="ExternalInput")
    wo_d = nc.dram_tensor("wo", [D, D], F32, kind="ExternalInput")
    wf_d = nc.dram_tensor("wf", [D, D], F32, kind="ExternalInput")
    gam_d = nc.dram_tensor("gam", [1, D], F32, kind="ExternalInput")
    bet_d = nc.dram_tensor("bet", [1, D], F32, kind="ExternalInput")
    mask_d = nc.dram_tensor("mask01", [128, 128], BF16, kind="ExternalInput")
    idf_d = nc.dram_tensor("idf", [128, 128], F32, kind="ExternalInput")
    idb_d = nc.dram_tensor("idb", [128, 128], BF16, kind="ExternalInput")
    onesr_d = nc.dram_tensor("onesr", [1, 128], F32, kind="ExternalInput")
    y_d = nc.dram_tensor("y", [TPC, D], F32, kind="ExternalOutput")
    x2s_d = nc.dram_tensor("x2s", [TPC, D], F32, kind="Internal")
    xn2s_d = nc.dram_tensor("xn2s", [TPC, D], F32, kind="Internal")
    cc_in_d = nc.dram_tensor("cc_in", [1, 16], F32, kind="Internal")
    cc_out_d = nc.dram_tensor("cc_out", [1, 16], F32, kind="Internal",
                              addr_space="Shared")

    A = nc.alloc_sbuf_tensor
    mask_sb = A("mask_sb", [128, 128], BF16)
    idf_sb = A("idf_sb", [128, 128], F32)
    idb_sb = A("idb_sb", [128, 128], BF16)
    onesr_sb = A("onesr_sb", [1, 128], F32)
    gam_sb = A("gam_sb", [1, D], F32)
    bet_sb = A("bet_sb", [1, D], F32)
    gb_sb = A("gb_sb", [128, D], F32)
    bb_sb = A("bb_sb", [128, D], F32)
    wqt_sb = A("wqt_sb", [128, 8 * D], BF16)
    bn12 = A("bn12", [128, 12], F32)
    bn2 = A("bn2", [128, 2], F32)
    vp = A("vp", [128, 1], F32)
    iv = A("iv", [128, 1], F32)
    rr = A("rr", [128, 1], F32)
    bcol = A("bcol", [128, 1], F32)
    bn12b = A("bn12b", [128, 12], F32)
    bn2b = A("bn2b", [128, 2], F32)
    vpb = A("vpb", [128, 1], F32)
    ivb = A("ivb", [128, 1], F32)
    rrb = A("rrb", [128, 1], F32)
    bcolb = A("bcolb", [128, 1], F32)
    mxcol = A("mxcol", [128, 1], F32)
    tmax = A("tmax", [128, 1], F32)
    tmax16 = A("tmax16", [128, 16], F32)
    SA = [dict(s6=A(f"sa6_{i}", [128, 12], F32), s2=A(f"sa2_{i}", [128, 2], F32),
               vp=A(f"savp_{i}", [128, 1], F32), iv=A(f"saiv_{i}", [128, 1], F32),
               rt=A(f"sar_{i}", [128, 1], F32), bt=A(f"sab_{i}", [128, 1], F32))
          for i in range(2)]
    SB = [dict(s6=A(f"sb6_{i}", [128, 12], F32), s2=A(f"sb2_{i}", [128, 2], F32),
               vp=A(f"sbvp_{i}", [128, 1], F32), iv=A(f"sbiv_{i}", [128, 1], F32),
               rt=A(f"sbr_{i}", [128, 1], F32), bt=A(f"sbb_{i}", [128, 1], F32))
          for i in range(2)]
    rc_sb = A("rc_sb", [128, 7], F32)
    ones_col = A("ones_col", [128, 1], F32)
    ws8 = A("ws8", [128, 8], F32)
    wm8 = A("wm8", [128, 8], F32)
    wc8 = A("wc8", [128, 8], F32)
    dcol = A("dcol", [128, 1], F32)
    ndcol = A("ndcol", [128, 1], F32)
    srow = A("srow", [1, 12], F32)
    gmr = A("gmr", [1, 4], F32)
    alpha = A("alpha", [1, 1], F32)
    invs_col = A("invs_col", [128, 1], F32)
    sa_col = A("sa_col", [128, 1], F32)

    AR, off = {}, 0

    def region(name, size):
        nonlocal off
        AR[name] = off
        off += (size + 31) & ~31

    region("xT", 8192)
    region("xn", 32768)
    region("wv0", 16384)
    region("wv1", 16384)
    region("K", 32768)
    region("V", 33280)
    region("Q", 32768)
    ARENA = off
    arena = A("arena", [128, ARENA], U8)
    abase = nc.lookup_mloc(arena).addr

    def place(name, shape, dtype, reg, extra=0):
        hnd = nc.virtual_sbuf_tensor(name, list(shape), dtype)
        nc._tensor(hnd.name, list(hnd.shape), hnd.dtype, type="SB")
        mloc = nc.lookup_mloc(hnd)
        mloc.allocated = True
        mloc.addr = abase + AR[reg] + extra
        return hnd

    xT = [place(f"xT{i}", [128, 1024], F32, "xT", 4096 * i) for i in range(2)]
    xn_fm = place("xn_fm", [128, 16384], BF16, "xn")
    attn_tm = place("attn_tm", [128, 16384], BF16, "xn")
    wv0 = place("wv0", [128, 8192], BF16, "wv0")
    wo_sb = place("wo_sb", [128, 8192], BF16, "wv0")
    wv1 = place("wv1", [128, 8192], BF16, "wv1")
    E_sb = place("E_sb", [128, 6144], BF16, "wv1")
    ternB = place("ternB", [128, 1024], BF16, "wv1", 12288)
    ternB2 = place("ternB2", [128, 1024], BF16, "K", 28672)
    yO2 = [place("yO", [128, 1024], F32, "wv1", 0),
           place("yO1", [128, 1024], F32, "Q", 0)]
    hO2 = [place("hO", [128, 1024], F32, "wv1", 4096),
           place("hO1", [128, 1024], F32, "Q", 4096)]
    K_fm = place("K_fm", [128, 16384], BF16, "K")
    x2O = [place(f"x2O{i}", [128, 1024], F32, "K", 4096 * i) for i in range(2)]
    wfb = [place(f"wfb{i}", [128, 1024], F32, "K", 8192 + 4096 * i) for i in range(2)]
    g1W = place("g1W", [128, 1024], F32, "K", 16384)
    g2W = place("g2W", [128, 1024], F32, "K", 20480)
    ternW = place("ternW", [128, 1024], F32, "K", 24576)
    t1O2 = [place("t1O", [128, 1024], F32, "K", 28672),
            place("t1O1", [128, 1024], F32, "Q", 8192)]
    V_aug = place("V_aug", [128, 16640], BF16, "V")
    attn_fm = place("attn_fm", [128, 16384], BF16, "V")
    Q_fm = place("Q_fm", [128, 16384], BF16, "Q")
    xnC = [place(f"xnC{i}", [128, 1024], F32, "Q", 4096 * i) for i in range(2)]
    t1C = place("t1C", [128, 1024], F32, "Q", 8192)
    intsT = [place(f"intsT{i}", [128, 1024], BF16, "Q", 12288 + 2048 * i) for i in range(2)]
    intsF = [place(f"intsF{i}", [128, 1024], BF16, "Q", 16384 + 2048 * i) for i in range(2)]
    outC = [place(f"outC{i}", [128, 1024], F32, "Q", 20480 + 4096 * i) for i in range(2)]

    psS = [nc.alloc_psum_tensor(f"psS{i}", [128, 1024], F32) for i in range(2)]
    psPV = [nc.alloc_psum_tensor(f"psPV{i}", [128, 512], F32) for i in range(3)]
    psX = nc.alloc_psum_tensor("psX", [128, 512], F32)

    s_in = Sem(nc, "s_in")
    s_w = Sem(nc, "s_w")
    s_tr = Sem(nc, "s_tr")
    s_mm = Sem(nc, "s_mm")
    s_v = Sem(nc, "s_v")
    s_a = Sem(nc, "s_a")
    s_st = Sem(nc, "s_st")
    s_cc = Sem(nc, "s_cc")
    s_g = Sem(nc, "s_g")
    s_y = Sem(nc, "s_y")

    sched = {k: [] for k in ("sync", "tensor", "vector", "scalar", "gpsimd")}

    def emit(eng, fn, waits=(), incs=()):
        waits = tuple((s.h, int(v)) for (s, v) in waits if v > 0)
        incs = tuple((s.h, int(k)) for (s, k) in incs)

        def run(e, fn=fn, waits=waits, incs=incs):
            for h, v in waits:
                e.wait_ge(h, v)
            inst = fn(e)
            for h, k in incs:
                inst.then_inc(h, k)
        sched[eng].append(run)

    def w(sem):
        return (sem, sem.n)

    def vwait(sem):
        return [(sem, sem.n)]

    tp_state = {"g": 0, "hist": []}

    def transpose_blocks(src_fn, dst_fn, n, first_waits, evac="vector"):
        first = True
        for g0 in range(0, n, 4):
            cnt = min(4, n - g0)
            buf = psX if tp_state["g"] % 2 == 0 else psPV[0]
            tp_state["g"] += 1
            guard = tp_state["hist"][-2] if len(tp_state["hist"]) >= 2 else None
            for j in range(g0, g0 + cnt):
                wl = []
                if j == g0:
                    wl = [guard] if guard else []
                    if first:
                        wl += list(first_waits)
                        first = False
                emit("tensor", lambda e, j=j, g0=g0, buf=buf: e.matmul(
                    buf[:, 128 * (j - g0):128 * (j - g0) + 128],
                    src_fn(j), idb_sb[:, :], start=True, stop=True),
                    waits=wl, incs=([s_mm.bump()] if j == g0 + cnt - 1 else []))
            if evac == "vector":
                emit("vector", lambda e, g0=g0, cnt=cnt, buf=buf: e.tensor_copy(
                    dst_fn(g0, cnt),
                    buf.ap()[:, 0:128 * cnt].rearrange("p (b r) -> p b r", r=128)),
                    waits=vwait(s_mm), incs=[s_v.bump()])
                tp_state["hist"].append((s_v, s_v.n))
            else:
                emit("scalar", lambda e, g0=g0, cnt=cnt, buf=buf: e.activation(
                    dst_fn(g0, cnt),
                    buf.ap()[:, 0:128 * cnt].rearrange("p (b r) -> p b r", r=128),
                    AF.Identity, bias=np.float32(0.0), scale=np.float32(1.0)),
                    waits=vwait(s_mm), incs=[s_a.bump()])
                tp_state["hist"].append((s_a, s_a.n))
        return tp_state["hist"][-1]

    for dst, src in ((mask_sb, mask_d), (idf_sb, idf_d), (idb_sb, idb_d),
                     (onesr_sb, onesr_d), (gam_sb, gam_d), (bet_sb, bet_d)):
        emit("sync", lambda e, d=dst, s=src: e.dma_start(out=d[:, :], in_=s[:, :]),
             incs=[s_in.bump(16)])
    consts_in = s_in.n
    for t, val in ((ones_col, 1.0), (mxcol, 0.0)):
        emit("vector", lambda e, t=t, v=val: e.memset(t[:, :], v))
    emit("vector", lambda e: e.memset(
        V_aug.ap().rearrange("p (t r) -> p t r", t=16)[:, :, 64:1040:65], 1.0))
    emit("vector", lambda e: e.drain(), incs=[s_v.bump()])
    memset_v = s_v.n
    for which, (src, dst) in enumerate(((gam_sb, gb_sb), (bet_sb, bb_sb))):
        for hf in range(2):
            emit("tensor", lambda e, s=src, hf=hf: e.matmul(
                psX[:, 0:512], onesr_sb[:, :], s[0:1, 512 * hf:512 * hf + 512],
                start=True, stop=True),
                waits=[(s_in, consts_in)] + ([] if which == 0 and hf == 0 else vwait(s_v)),
                incs=[s_mm.bump()])
            emit("vector", lambda e, d=dst, hf=hf: e.tensor_copy(
                d[:, 512 * hf:512 * hf + 512], psX[:, 0:512]),
                waits=vwait(s_mm), incs=[s_v.bump()])
    gbeta_v = s_v.n

    wkv3 = wkv_d.ap().rearrange("(c p) i -> p c i", p=128)
    wq3 = wq_d.ap().rearrange("(c p) i -> p c i", p=128)
    wo3 = wo_d.ap().rearrange("(c p) i -> p c i", p=128)
    wf3 = wf_d.ap().rearrange("(c p) i -> p c i", p=128)

    def stats_part1(src_ap, S, extra_waits=(), eps_mul=None):
        s6, s2, vpt, ivt, rt = (S[k] for k in ("s6", "s2", "vp", "iv", "rt"))
        emit("vector", lambda e, a=s6, b=src_ap: e.bn_stats(a[:, 0:6], b[:, 0:512]),
             waits=list(extra_waits))
        emit("vector", lambda e, a=s6, b=src_ap: e.bn_stats(a[:, 6:12], b[:, 512:1024]))
        emit("vector", lambda e: e.drain())
        emit("vector", lambda e, a=s2, b=s6: e.bn_aggr(a[:, :], b[:, :]))
        emit("vector", lambda e: e.drain())
        if eps_mul is None:
            emit("vector", lambda e, a=vpt, b=s2: e.tensor_scalar(
                a[:, :], b[:, 1:2], EPS, None, ALU.add))
        else:
            m, cadd = eps_mul
            emit("vector", lambda e, a=vpt, b=s2, m=m, c=cadd: e.tensor_scalar(
                a[:, :], b[:, 1:2], m, c, ALU.mult, ALU.add))
        emit("vector", lambda e: e.drain())
        emit("vector", lambda e, a=ivt, b=vpt: e.reciprocal(a[:, :], b[:, :]),
             incs=[s_v.bump()])
        emit("scalar", lambda e, a=rt, b=ivt: e.activation(a[:, :], b[:, :], AF.Sqrt),
             waits=vwait(s_v), incs=[s_a.bump()])
        return s_a.n

    def stats_part2(S, sqrt_a):
        s2, rt, bt = S["s2"], S["rt"], S["bt"]
        emit("vector", lambda e, a=bt, b=s2, c=rt: e.scalar_tensor_tensor(
            a[:, :], b[:, 0:1], -1.0, c[:, :], ALU.mult, ALU.mult),
            waits=[(s_a, sqrt_a)])
        emit("vector", lambda e: e.drain(), incs=[s_v.bump()])

    def stats_block2(src_ap, S, extra_waits=()):
        sa = stats_part1(src_ap, S, extra_waits)
        stats_part2(S, sa)

    def stats_block(src_ap, s6, s2, vpt, ivt, rt, bt, extra_waits=()):
        stats_block2(src_ap, dict(s6=s6, s2=s2, vp=vpt, iv=ivt, rt=rt, bt=bt),
                     extra_waits)

    o_evac_last = [None]
    for sq in range(2):
        tbase = NTS * sq

        vec_now = s_v.n
        emit("sync", lambda e, t=tbase: e.dma_start(
            out=xT[0][:, :], in_=x_d[128 * t:128 * (t + 1), :]),
            waits=[(s_v, vec_now)], incs=[s_in.bump(16)])
        emit("gpsimd", lambda e: e.dma_start(
            out=wv0.ap().rearrange("p (c i) -> p c i", c=8),
            in_=wkv3[:, :, 0:1024]),
            waits=[(s_mm, s_mm.n)], incs=[s_w.bump(16)])
        wk_ready = s_w.n

        evac_hist = []
        base_v = s_v.n
        base_st = s_st.n
        base_a = s_a.n
        slot_ctr = [0]

        def emit_proj_group(dst, w_sb, w_ready_val, tb, kc, extra):
            slot = slot_ctr[0]
            slot_ctr[0] += 1
            p = psS[slot % 2][:, 512 * ((slot // 2) % 2):
                              512 * ((slot // 2) % 2) + 512]
            guard = evac_hist[-4] if len(evac_hist) >= 4 else base_v
            for c in range(8):
                wl = []
                if c == 0:
                    wl = [(s_w, w_ready_val), (s_v, guard)] + extra
                emit("tensor", lambda e, kc=kc, tb=tb, c=c, p=p, ws=w_sb: e.matmul(
                    p, ws[:, 1024 * c + 128 * kc:1024 * c + 128 * kc + 128],
                    xn_fm[:, 2048 * c + 512 * tb:2048 * c + 512 * tb + 512],
                    start=(c == 0), stop=(c == 7)),
                    waits=wl, incs=([s_mm.bump()] if c == 7 else []))
            emit("vector", lambda e, kc=kc, tb=tb, p=p, d=dst: e.tensor_copy(
                d[:, 2048 * kc + 512 * tb:2048 * kc + 512 * tb + 512], p),
                waits=vwait(s_mm)
                + ([(s_st, base_st), (s_a, base_a)]
                   if len(evac_hist) == 0 else []),
                incs=[s_v.bump()])
            evac_hist.append(s_v.n)

        kq = [(tb, kc) for tb in range(4) for kc in range(8)]
        kq_i = [0]
        tr_ev_tile = {}
        xn_act_hist = {}
        tr_hist = {}
        for tt in range(NTS):
            buf = tt % 2
            xt_wait = s_in.n
            if tt + 1 < NTS:
                g = xn_act_hist.get((tt + 1) % 2, 0)
                emit("sync", lambda e, t=tbase + tt + 1, b=(tt + 1) % 2: e.dma_start(
                    out=xT[b][:, :], in_=x_d[128 * t:128 * (t + 1), :]),
                    waits=[(s_a, g), (s_v, vec_now)], incs=[s_in.bump(16)])
            stats_block(xT[buf], bn12, bn2, vp, iv, rr, bcol,
                        extra_waits=[(s_in, xt_wait)]
                        + ([(s_v, gbeta_v)] if tt == 0 and sq == 0 else []))
            emit("scalar", lambda e, b=buf: e.activation(
                E_sb[:, 2048 * b:2048 * b + 1024], xT[b][:, :], AF.Identity,
                bias=bcol[:, :], scale=rr[:, :]),
                waits=vwait(s_v) + ([(s_mm, tr_hist[buf])] if buf in tr_hist else [])
                + ([(s_g, s_g.n)] if tt == 0 else []),
                incs=[s_a.bump()])
            xn_act_hist[buf] = s_a.n
            tr_ev_tile[tt] = transpose_blocks(
                lambda j, b=buf: E_sb[:, 2048 * b + 128 * j:2048 * b + 128 * j + 128],
                lambda g0, cnt, t=tt: xn_fm.ap().rearrange(
                    "p (c r) -> p c r", c=8)[:, g0:g0 + cnt,
                                             128 * t:128 * t + 128],
                8, [(s_a, xn_act_hist[buf])])
            tr_hist[buf] = s_mm.n
            ready = 8 * ((tt - 3) // 4 + 1) if tt >= 3 else 0
            budget = 3 if sq == 1 else 0
            while budget > 0 and kq_i[0] < ready:
                tb, kc = kq[kq_i[0]]
                emit_proj_group(K_fm, wv0, wk_ready, tb, kc,
                                [tr_ev_tile[4 * tb + 3]] if kc == 0 else [])
                kq_i[0] += 1
                budget -= 1
        while kq_i[0] < len(kq):
            tb, kc = kq[kq_i[0]]
            emit_proj_group(K_fm, wv0, wk_ready, tb, kc,
                            [tr_ev_tile[4 * tb + 3]] if kc == 0 else [])
            kq_i[0] += 1
        xn_tr_v = s_v.n

        emit("gpsimd", lambda e: e.dma_start(
            out=wv1.ap().rearrange("p (c i) -> p c i", c=8),
            in_=wkv3[:, :, 1024:2048]),
            waits=[tr_ev_tile[NTS - 1]], incs=[s_w.bump(16)])
        wv_ready = s_w.n

        emit("gpsimd", lambda e: e.dma_start(
            out=wv0.ap().rearrange("p (c i) -> p c i", c=8), in_=wq3),
            waits=[(s_mm, s_mm.n)], incs=[s_w.bump(16)])
        wq_ready = s_w.n

        emit("vector", lambda e: e.memset(
            V_aug.ap().rearrange("p (t r) -> p t r", t=16)[:, :, 64:1040:65], 1.0),
            waits=[(s_mm, s_mm.n)] + ([(s_v, memset_v)] if sq == 0 else [])
            + ([o_evac_last[0]] if o_evac_last[0] else []))
        emit("vector", lambda e: e.drain(), incs=[s_v.bump()])
        ones_v = s_v.n
        for tt in range(NTS):
            p = psS[tt % 2]
            guard = evac_hist[-2] if len(evac_hist) >= 2 else base_v
            for c in range(8):
                for hf in range(2):
                    wl = []
                    if c == 0 and hf == 0:
                        wl = [(s_w, wv_ready), (s_v, guard)]
                    emit("tensor", lambda e, t=tt, c=c, hf=hf, p=p: e.matmul(
                        p[:, 512 * hf:512 * hf + 512],
                        xn_fm[:, 2048 * c + 128 * t:2048 * c + 128 * t + 128],
                        wv1[:, 1024 * c + 512 * hf:1024 * c + 512 * hf + 512],
                        start=(c == 0), stop=(c == 7)),
                        waits=wl,
                        incs=([s_mm.bump()] if c == 7 and hf == 1 else []))
            emit("vector", lambda e, t=tt, p=p: e.tensor_copy(
                V_aug[:, 1040 * t:1040 * t + 1040]
                     .rearrange("p (h r) -> p h r", r=65)[:, :, 0:64],
                p.ap().rearrange("p (h r) -> p h r", h=16)),
                waits=vwait(s_mm) + ([(s_v, ones_v)] if tt == 0 else []),
                incs=[s_v.bump()])
            evac_hist.append(s_v.n)

        for tb in range(4):
            for kc in range(8):
                emit_proj_group(Q_fm, wv0, wq_ready, tb, kc, [])
        q_done_v = s_v.n
        q_done_mm = s_mm.n

        emit("gpsimd", lambda e: e.dma_start(
            out=wo_sb.ap().rearrange("p (c i) -> p c i", c=8), in_=wo3),
            waits=[(s_mm, q_done_mm)], incs=[s_w.bump(16)])
        wo_ready = s_w.n

        steps = []
        for h in range(H):
            for pas in range(2):
                for jj in (range(8) if pas == 0 else range(16)):
                    steps.append((h, pas, jj))
        exp_cnt = []
        mm_step = []
        pv_ev = s_v.n
        head_started_banks = {}
        head_first_pv = {}

        def emit_S(k):
            h, pas, jj = steps[k]
            hc, po = h // 2, 64 * (h % 2)
            ps = psS[k % 2]
            qlo = max(128 * jj, 1024 * pas)
            qhi = 1024 * (pas + 1)
            q0 = qlo
            first = True
            while q0 < qhi:
                q1 = min(q0 + 512, qhi,
                         1024 * pas + ((q0 - 1024 * pas) // 512 + 1) * 512)
                wl = []
                if first:
                    if k >= 2:
                        wl.append((s_a, exp_cnt[k - 2]))
                    else:
                        wl.append((s_v, q_done_v))
                    first = False
                emit("tensor", lambda e, q0=q0, q1=q1, p=ps, hc=hc,
                     po=po, jj=jj, pas=pas: e.matmul(
                         p[:, q0 - 1024 * pas:q1 - 1024 * pas],
                         K_fm[po:po + 64,
                              2048 * hc + 128 * jj:2048 * hc + 128 * jj + 128],
                         Q_fm[po:po + 64, 2048 * hc + q0:2048 * hc + q1],
                         start=True, stop=True),
                     waits=wl,
                     incs=([s_mm.bump()] if q1 == qhi else []))
                q0 = q1
            mm_step.append(s_mm.n)

        msk_of = {}

        def emit_exp_mask(k):
            h, pas, jj = steps[k]
            ps = psS[k % 2]
            eoff = 2048 * (k % 3)
            qlo = max(128 * jj, 1024 * pas)
            qhi = 1024 * (pas + 1)
            emit("scalar", lambda e, qlo=qlo, qhi=qhi, p=ps, eoff=eoff,
                 pas=pas: e.activation(
                     E_sb[:, eoff + qlo - 1024 * pas:eoff + qhi - 1024 * pas],
                     p[:, qlo - 1024 * pas:qhi - 1024 * pas],
                     AF.Exp, bias=np.float32(EXP_C), scale=np.float32(0.125)),
                 waits=[(s_mm, mm_step[k])], incs=[s_a.bump()])
            exp_cnt.append(s_a.n)
            if 1024 * pas <= 128 * jj < 1024 * (pas + 1):
                emit("vector", lambda e, jj=jj, eoff=eoff, pas=pas: e.tensor_tensor(
                    E_sb[:, eoff + 128 * jj - 1024 * pas:
                         eoff + 128 * jj - 1024 * pas + 128],
                    E_sb[:, eoff + 128 * jj - 1024 * pas:
                         eoff + 128 * jj - 1024 * pas + 128],
                    mask_sb[:, :], ALU.mult),
                    waits=[(s_a, exp_cnt[k])], incs=[s_v.bump()])
                msk_of[k] = s_v.n

        def emit_PV(k):
            nonlocal pv_ev
            h, pas, jj = steps[k]
            eoff = 2048 * (k % 3)
            ilo = max(jj, 8 * pas)
            ihi = 8 * (pas + 1)
            order = [i for i in range(ilo, ihi) if i != jj]
            if ilo <= jj < ihi:
                order.append(jj)
            started = head_started_banks.setdefault(h, set())
            for n_i, i in enumerate(order):
                bnk, sl = i // 7, i % 7
                wl = []
                if n_i == 0:
                    wl.append((s_a, exp_cnt[k]))
                    if not head_first_pv.get(h):
                        wl.append((s_v, pv_ev))
                        head_first_pv[h] = True
                if k in msk_of and i == jj:
                    wl.append((s_v, msk_of[k]))
                last = (jj == i)
                fin = last and (i in (6, 13, 15))
                st_flag = bnk not in started
                if st_flag:
                    started.add(bnk)
                emit("tensor", lambda e, i=i, jj=jj, h=h, eoff=eoff,
                     bnk=bnk, sl=sl, pas=pas, last=last, st=st_flag: e.matmul(
                         psPV[bnk][:, 65 * sl:65 * sl + 65],
                         E_sb[:, eoff + 128 * i - 1024 * pas:
                              eoff + 128 * i - 1024 * pas + 128],
                         V_aug[:, 1040 * jj + 65 * h:1040 * jj + 65 * h + 65],
                         start=st, stop=last),
                     waits=wl,
                     incs=([s_mm.bump()] if fin else []))
            if pas == 1 and jj == 15:
                base_mm = s_mm.n - 3
                for bnk, ntile in ((0, 7), (1, 7), (2, 2)):
                    emit("vector", lambda e, bnk=bnk, nt=ntile: e.reciprocal(
                        rc_sb[:, 0:nt], psPV[bnk][:, 64:64 + (nt - 1) * 65 + 1:65]),
                        waits=[(s_mm, base_mm + bnk + 1)])
                    emit("vector", lambda e: e.drain())
                    emit("vector", lambda e, bnk=bnk, nt=ntile, h=h: e.tensor_tensor(
                        attn_tm.ap().rearrange("p (t r) -> p t r", r=1024)
                               [:, 7 * bnk:7 * bnk + nt, 64 * h:64 * h + 64],
                        psPV[bnk][:, 0:nt * 65].rearrange("p (t r) -> p t r", r=65)
                                [:, :, 0:64],
                        rc_sb[:, 0:nt].unsqueeze(2).broadcast_to([128, nt, 64]),
                        ALU.mult))
                    emit("vector", lambda e: e.drain(), incs=[s_v.bump()])
                pv_ev = s_v.n

        emit_S(0)
        for k in range(len(steps)):
            emit_exp_mask(k)
            if k + 1 < len(steps):
                emit_S(k + 1)
            if k >= 1:
                emit_PV(k - 1)
        emit_PV(len(steps) - 1)
        attn_v = s_v.n
        attn_mm = s_mm.n
        exp_last = exp_cnt[-1]

        vec_now = s_v.n
        emit("sync", lambda e, t=tbase: e.dma_start(
            out=xT[0][:, :], in_=x_d[128 * t:128 * (t + 1), :]),
            waits=[(s_a, xn_act_hist[0])], incs=[s_in.bump(16)])
        x2v_hist, apply_v, st_hist, gh_hist = {}, {}, {}, {}
        h_v, mx_v_of, pend = {}, {}, {}
        mx_v_last = [0]
        tr_v_of = {}

        def o_transposes(t2, first):
            tr_v_of[t2] = transpose_blocks(
                lambda j, t=t2: attn_tm[:, 1024 * t + 128 * j:1024 * t + 128 * j + 128],
                lambda g0, cnt, t=t2: attn_fm.ap().rearrange(
                    "p (c r) -> p c r", c=8)[:, g0:g0 + cnt,
                                             128 * t:128 * t + 128],
                8, [(s_v, attn_v)] if first else [], evac="scalar")

        def emit_redmax(pt):
            b = pt % 2
            emit("vector", lambda e, b=b, pt=pt: e.reduce_max(
                tmax16[:, pt:pt + 1], t1O2[b][:, :], axis=AX.X,
                apply_absolute_value=True),
                waits=[(s_g, gh_hist[(pt, "g")])], incs=[s_v.bump()])
            mx_v_of[b] = s_v.n
            mx_v_last[0] = s_v.n

        def apply_unit(pt):
            b = pt % 2
            Sa = SA[b]
            sa_a = pend[b]
            stats_part2(Sa, sa_a)
            emit("vector", lambda e, b=b, Sa=Sa: e.tensor_scalar(
                hO2[b][:, :], x2O[b][:, :], Sa["rt"][:, :], Sa["bt"][:, :],
                ALU.mult, ALU.add),
                waits=[(s_g, gh_hist.get(b, 0))], incs=[s_v.bump()])
            h_v[b] = apply_v[b] = s_v.n
            emit("gpsimd", lambda e, b=b: e.tensor_mul(
                t1O2[b][:, :], hO2[b][:, :], gb_sb[:, :]),
                waits=[(s_v, h_v[b]), (s_st, st_hist.get(("xn2", b), 0)),
                       (s_v, mx_v_of.get(b, 0))])
            emit("gpsimd", lambda e, b=b: e.tensor_add(
                t1O2[b][:, :], t1O2[b][:, :], bb_sb[:, :]), incs=[s_g.bump()])
            gh_hist[b] = gh_hist[(pt, "g")] = s_g.n
            emit("sync", lambda e, t=tbase + pt, b=b: e.dma_start(
                out=xn2s_d[128 * t:128 * (t + 1), :], in_=t1O2[b][:, :]),
                waits=[(s_g, gh_hist[b])], incs=[s_st.bump(16)])
            st_hist[("xn2", b)] = s_st.n
            if pt > 0:
                emit_redmax(pt - 1)

        o_transposes(0, True)
        for tt in range(NTS):
            buf = tt % 2
            xt_wait = s_in.n
            if tt + 1 < NTS:
                g = x2v_hist.get((tt + 1) % 2, 0)
                ga = xn_act_hist[(tt + 1) % 2] if tt == 0 else 0
                emit("sync", lambda e, t=tbase + tt + 1, b=(tt + 1) % 2,
                     g=g, ga=ga: e.dma_start(
                         out=xT[b][:, :], in_=x_d[128 * t:128 * (t + 1), :]),
                     waits=[(s_v, g), (s_a, ga)], incs=[s_in.bump(16)])
                o_transposes(tt + 1, False)
            p = psS[tt % 2]
            for c in range(8):
                for hf in range(2):
                    wl = []
                    if c == 0 and hf == 0:
                        wl = [tr_v_of[tt], (s_w, wo_ready)]
                        if tt < 2:
                            wl.append((s_a, exp_last))
                        else:
                            wl.append((s_v, x2v_hist[tt % 2]))
                    emit("tensor", lambda e, t=tt, c=c, hf=hf, p=p: e.matmul(
                        p[:, 512 * hf:512 * hf + 512],
                        attn_fm[:, 2048 * c + 128 * t:2048 * c + 128 * t + 128],
                        wo_sb[:, 1024 * c + 512 * hf:1024 * c + 512 * hf + 512],
                        start=(c == 0), stop=(c == 7)),
                        waits=wl,
                        incs=([s_mm.bump()] if c == 7 and hf == 1 else []))
            emit("vector", lambda e, b=buf, p=p: e.tensor_add(
                x2O[b][:, :], p[:, :], xT[b][:, :]),
                waits=vwait(s_mm) + [(s_in, xt_wait),
                                     (s_st, st_hist.get(buf, 0))])
            emit("vector", lambda e: e.drain(), incs=[s_v.bump()])
            x2v_hist[buf] = s_v.n
            emit("sync", lambda e, t=tbase + tt, b=buf: e.dma_start(
                out=x2s_d[128 * t:128 * (t + 1), :], in_=x2O[b][:, :]),
                waits=[(s_v, x2v_hist[buf])], incs=[s_st.bump(16)])
            st_hist[buf] = s_st.n
            pend[buf] = stats_part1(x2O[buf], SA[buf],
                                    eps_mul=(1.0 + EPS, EPS * EPS))
            if tt > 0:
                apply_unit(tt - 1)
        apply_unit(NTS - 1)
        emit_redmax(NTS - 1)
        emit("vector", lambda e: e.drain())
        emit("vector", lambda e: e.reduce_max(tmax[:, :], tmax16[:, :], axis=AX.X))
        emit("vector", lambda e: e.drain())
        emit("vector", lambda e: e.tensor_max(mxcol[:, :], mxcol[:, :], tmax[:, :]))
        emit("vector", lambda e: e.drain(), incs=[s_v.bump()])
        mx_v_last[0] = s_v.n
        o_evac_last[0] = tr_v_of[NTS - 1]

        if sq == 0:
            emit("gpsimd", lambda e: e.dma_start(out=wfb[0][:, :], in_=wf3[:, 0, :]),
                 waits=[(s_v, attn_v)], incs=[s_w.bump(16)])
            wf_rd = {}
            for c in range(8):
                wf_wait = s_w.n
                if c + 1 < 8:
                    emit("gpsimd", lambda e, c=c: e.dma_start(
                        out=wfb[(c + 1) % 2][:, :], in_=wf3[:, c + 1, :]),
                        waits=[(s_v, wf_rd.get((c + 1) % 2, 0))],
                        incs=[s_w.bump(16)])
                emit("vector", lambda e, c=c: e.reduce_sum(
                    ws8[:, c:c + 1], wfb[c % 2][:, :], axis=AX.X,
                    apply_absolute_value=True),
                    waits=[(s_w, wf_wait)], incs=[s_v.bump()])
                wf_rd[c % 2] = s_v.n
            emit("tensor", lambda e: e.matmul(
                psX[0:1, 0:8], ones_col[:, :], ws8[:, :], start=True, stop=True),
                waits=vwait(s_v), incs=[s_mm.bump()])
            emit("vector", lambda e: e.reduce_sum(srow[0:1, 0:1], psX[0:1, 0:8],
                                                  axis=AX.X),
                 waits=vwait(s_mm))
            emit("vector", lambda e: e.drain())
            emit("vector", lambda e: e.tensor_scalar(
                srow[0:1, 1:2], srow[0:1, 0:1], 0.7 / (1024.0 * 1024.0), None, ALU.mult))
            emit("vector", lambda e: e.drain(), incs=[s_v.bump()])
            emit("tensor", lambda e: e.matmul(
                psX[:, 128:129], onesr_sb[:, :], srow[0:1, 1:2], start=True, stop=True),
                waits=vwait(s_v), incs=[s_mm.bump()])
            emit("vector", lambda e: e.tensor_copy(dcol[:, :], psX[:, 128:129]),
                 waits=vwait(s_mm))
            emit("vector", lambda e: e.drain())
            emit("vector", lambda e: e.tensor_scalar(ndcol[:, :], dcol[:, :], -1.0,
                                                     None, ALU.mult))
            emit("vector", lambda e: e.drain(), incs=[s_v.bump()])
            dcol_v = s_v.n
            emit("gpsimd", lambda e: e.dma_start(out=wfb[0][:, :], in_=wf3[:, 0, :]),
                 waits=[(s_v, dcol_v)], incs=[s_w.bump(16)])
            tr_mm_of = {}
            tern_a_of = {}
            wf_rd = {}
            tB = [ternB, ternB2]
            st_xn2_0 = st_hist.get(("xn2", 0), 0)
            for c in range(8):
                wf_wait = s_w.n
                if c + 1 < 8:
                    emit("gpsimd", lambda e, c=c: e.dma_start(
                        out=wfb[(c + 1) % 2][:, :], in_=wf3[:, c + 1, :]),
                        waits=[(s_v, wf_rd.get((c + 1) % 2, 0))],
                        incs=[s_w.bump(16)])
                b = c % 2
                emit("vector", lambda e, b=b: e.tensor_scalar(
                    g1W[:, :], wfb[b][:, :], dcol[:, :], None, ALU.is_gt),
                    waits=[(s_w, wf_wait)])
                emit("vector", lambda e, b=b: e.tensor_scalar(
                    g2W[:, :], wfb[b][:, :], ndcol[:, :], None, ALU.is_lt))
                emit("vector", lambda e: e.drain())
                emit("vector", lambda e: e.tensor_sub(ternW[:, :], g1W[:, :], g2W[:, :]),
                     waits=([(s_a, tern_a_of[c - 1])] if c > 0 else []))
                emit("vector", lambda e: e.drain(), incs=[s_v.bump()])
                tern_w_v = s_v.n
                emit("vector", lambda e, c=c: e.reduce_sum(
                    wc8[:, c:c + 1], ternW[:, :], axis=AX.X,
                    apply_absolute_value=True))
                emit("vector", lambda e, b=b: e.tensor_mul(g1W[:, :], wfb[b][:, :],
                                                           ternW[:, :]))
                emit("vector", lambda e: e.drain())
                emit("vector", lambda e, c=c: e.reduce_sum(
                    wm8[:, c:c + 1], g1W[:, :], axis=AX.X), incs=[s_v.bump()])
                wf_rd[b] = s_v.n
                emit("scalar", lambda e, b=b: e.activation(
                    tB[b][:, :], ternW[:, :], AF.Identity,
                    bias=np.float32(0.0), scale=np.float32(1.0)),
                    waits=[(s_v, tern_w_v)]
                    + ([(s_mm, tr_mm_of[c - 2])] if c > 1 else [])
                    + ([(s_st, st_xn2_0)] if c == 1 else []),
                    incs=[s_a.bump()])
                tern_a_of[c] = s_a.n
                transpose_blocks(
                    lambda ic, b=b: tB[b][:, 128 * ic:128 * ic + 128],
                    lambda g0, cnt, c=c: wqt_sb.ap().rearrange(
                        "p (ic r) -> p ic r", ic=8)[:, g0:g0 + cnt,
                                                    128 * c:128 * c + 128],
                    8, [(s_a, tern_a_of[c])], evac="scalar")
                tr_mm_of[c] = s_mm.n
            emit("tensor", lambda e: e.matmul(
                psX[0:1, 129:137], ones_col[:, :], wm8[:, :], start=True, stop=True),
                waits=vwait(s_v), incs=[s_mm.bump()])
            emit("tensor", lambda e: e.matmul(
                psX[0:1, 137:145], ones_col[:, :], wc8[:, :], start=False, stop=True),
                incs=[s_mm.bump()])
            emit("vector", lambda e: e.reduce_sum(
                srow[0:1, 8:9], psX[0:1, 129:137], axis=AX.X),
                waits=vwait(s_mm))
            emit("vector", lambda e: e.reduce_sum(
                srow[0:1, 9:10], psX[0:1, 137:145], axis=AX.X))
            emit("vector", lambda e: e.drain())
            emit("vector", lambda e: e.tensor_scalar(
                srow[0:1, 2:3], srow[0:1, 9:10], 1.0, None, ALU.max))
            emit("vector", lambda e: e.drain())
            emit("vector", lambda e: e.reciprocal(srow[0:1, 3:4], srow[0:1, 2:3]))
            emit("vector", lambda e: e.drain())
            emit("vector", lambda e: e.tensor_mul(alpha[0:1, :], srow[0:1, 8:9],
                                                  srow[0:1, 3:4]))
            emit("vector", lambda e: e.drain(), incs=[s_v.bump()])

    emit("tensor", lambda e: e.matmul(
        psX[0:1, 0:128], mxcol[:, :], idf_sb[:, :], start=True, stop=True),
        waits=vwait(s_v), incs=[s_mm.bump()])
    emit("vector", lambda e: e.reduce_max(gmr[0:1, 0:1], psX[0:1, 0:128], axis=AX.X),
         waits=vwait(s_mm))
    emit("vector", lambda e: e.drain(), incs=[s_v.bump()])
    emit("sync", lambda e: e.dma_start(out=cc_in_d[0:1, 0:1], in_=gmr[0:1, 0:1]),
         waits=vwait(s_v), incs=[s_st.bump(16)])
    emit("gpsimd", lambda e: e.collective_compute(
        "AllReduce", ALU.max, replica_groups=[list(range(N_CORES))],
        ins=[cc_in_d.ap().opt()], outs=[cc_out_d.ap().opt()]),
        waits=[w(s_st)], incs=[s_cc.bump(1)])
    emit("sync", lambda e: e.dma_start(out=gmr[0:1, 1:2], in_=cc_out_d[0:1, 0:1]),
         waits=[w(s_cc)], incs=[s_in.bump(16)])
    emit("vector", lambda e: e.tensor_scalar(gmr[0:1, 2:3], gmr[0:1, 1:2], 1e-8,
                                             None, ALU.max),
         waits=[w(s_in)])
    emit("vector", lambda e: e.drain())
    emit("vector", lambda e: e.reciprocal(gmr[0:1, 3:4], gmr[0:1, 2:3]))
    emit("vector", lambda e: e.drain())
    emit("vector", lambda e: e.tensor_scalar(srow[0:1, 4:5], gmr[0:1, 3:4], 127.0,
                                             None, ALU.mult))
    emit("vector", lambda e: e.tensor_scalar(srow[0:1, 5:6], gmr[0:1, 2:3], 1.0 / 127.0,
                                             None, ALU.mult))
    emit("vector", lambda e: e.drain())
    emit("vector", lambda e: e.tensor_mul(srow[0:1, 6:7], srow[0:1, 5:6], alpha[0:1, :]))
    emit("vector", lambda e: e.drain(), incs=[s_v.bump()])
    emit("tensor", lambda e: e.matmul(
        psX[:, 200:201], onesr_sb[:, :], srow[0:1, 4:5], start=True, stop=True),
        waits=vwait(s_v), incs=[s_mm.bump()])
    emit("tensor", lambda e: e.matmul(
        psX[:, 201:202], onesr_sb[:, :], srow[0:1, 6:7], start=False, stop=True),
        incs=[s_mm.bump()])
    emit("vector", lambda e: e.tensor_copy(invs_col[:, :], psX[:, 200:201]),
         waits=vwait(s_mm))
    emit("vector", lambda e: e.tensor_copy(sa_col[:, :], psX[:, 201:202]))
    emit("vector", lambda e: e.drain(), incs=[s_v.bump()])
    scale_v = s_v.n
    x2w_done = s_st.n

    emit("sync", lambda e: e.dma_start(out=xnC[0][:, :], in_=xn2s_d[0:128, :]),
         waits=[(s_st, x2w_done), (s_mm, attn_mm)], incs=[s_in.bump(16)])
    emit("scalar", lambda e: e.dma_start(out=xT[0][:, :], in_=x2s_d[0:128, :]),
         waits=[(s_st, x2w_done)], incs=[s_tr.bump(16)])
    in_x = {0: s_tr.n}
    ints_hist, out_hist, y_hist = {}, {}, {}
    trC_hist, mmC_hist = {}, {}

    def c_quant_tr(t, in_wait):
        b = t % 2
        emit("vector", lambda e, b=b: e.tensor_scalar(
            t1C[:, :], xnC[b][:, :], invs_col[:, :], MAGIC, ALU.mult, ALU.add),
            waits=[(s_in, in_wait), (s_v, scale_v)]
            + ([(s_mm, trC_hist[b][1])] if b in trC_hist else []))
        emit("vector", lambda e: e.drain())
        emit("vector", lambda e, b=b: e.tensor_scalar(
            intsT[b][:, :], t1C[:, :], -MAGIC, None, ALU.add))
        emit("vector", lambda e: e.drain(), incs=[s_v.bump()])
        ints_hist[b] = s_v.n
        trC_hist[b] = (transpose_blocks(
            lambda j, b=b: intsT[b][:, 128 * j:128 * j + 128],
            lambda g0, cnt, b=b: intsF[b][:, 128 * g0:128 * (g0 + cnt)]
                .rearrange("p (b2 r) -> p b2 r", r=128),
            8, [(s_v, ints_hist[b])], evac="scalar"), s_mm.n)

    c_quant_tr(0, s_in.n)
    for tt in range(NT):
        buf = tt % 2
        in_next = s_in.n
        if tt + 1 < NT:
            b2 = (tt + 1) % 2
            emit("sync", lambda e, t=tt + 1, b=b2, g=ints_hist.get(b2, 0): e.dma_start(
                out=xnC[b][:, :], in_=xn2s_d[128 * t:128 * (t + 1), :]),
                waits=[(s_v, g)], incs=[s_in.bump(16)])
            in_next = s_in.n
            emit("scalar", lambda e, t=tt + 1, b=b2, g=out_hist.get(b2, 0): e.dma_start(
                out=xT[b][:, :], in_=x2s_d[128 * t:128 * (t + 1), :]),
                waits=[(s_v, g)], incs=[s_tr.bump(16)])
            in_x[tt + 1] = s_tr.n
        p = psS[tt % 2]
        for ic in range(8):
            for hf in range(2):
                wl = []
                if ic == 0 and hf == 0:
                    wl = [trC_hist[buf][0], (s_v, out_hist.get(buf, 0))]
                emit("tensor", lambda e, b=buf, ic=ic, hf=hf, p=p: e.matmul(
                    p[:, 512 * hf:512 * hf + 512],
                    intsF[b][:, 128 * ic:128 * ic + 128],
                    wqt_sb[:, 1024 * ic + 512 * hf:1024 * ic + 512 * hf + 512],
                    start=(ic == 0), stop=(ic == 7)),
                    waits=wl,
                    incs=([s_mm.bump()] if ic == 7 and hf == 1 else []))
        mmC_hist[buf] = s_mm.n
        if tt + 1 < NT:
            c_quant_tr(tt + 1, in_next)
        emit("vector", lambda e, b=buf, p=p: e.scalar_tensor_tensor(
            outC[b][:, :], p[:, :], sa_col[:, :], xT[b][:, :], ALU.mult, ALU.add),
            waits=[(s_mm, mmC_hist[buf]), (s_y, y_hist.get(buf, 0)),
                   (s_tr, in_x[tt])])
        emit("vector", lambda e: e.drain(), incs=[s_v.bump()])
        out_hist[buf] = s_v.n
        emit("sync", lambda e, t=tt, b=buf: e.dma_start(
            out=y_d[128 * t:128 * (t + 1), :], in_=outC[b][:, :]),
            waits=[(s_v, out_hist[buf])], incs=[s_y.bump(16)])
        y_hist[buf] = s_y.n
    emit("sync", lambda e: e.nop(), waits=[w(s_y)])

    with nc.Block() as blk:
        blk.sync(lambda e: [f(e) for f in sched["sync"]])
        blk.tensor(lambda e: [f(e) for f in sched["tensor"]])
        blk.vector(lambda e: [f(e) for f in sched["vector"]])
        blk.scalar(lambda e: [f(e) for f in sched["scalar"]])
        blk.gpsimd(lambda e: [f(e) for f in sched["gpsimd"]])
    return nc


_NC_CACHE = None


def kernel(L0_attn_x, L0_attn_Wq, L0_attn_Wkv, L0_attn_Wo,
           L0_ffn_weight, L0_ffn_gamma, L0_ffn_beta):
    global _NC_CACHE
    x = np.ascontiguousarray(np.asarray(L0_attn_x, dtype=np.float32))
    wq = np.ascontiguousarray(np.asarray(L0_attn_Wq, dtype=np.float32))
    wkv = np.ascontiguousarray(np.asarray(L0_attn_Wkv, dtype=np.float32))
    wo = np.ascontiguousarray(np.asarray(L0_attn_Wo, dtype=np.float32))
    wf = np.ascontiguousarray(np.asarray(L0_ffn_weight, dtype=np.float32))
    gam = np.ascontiguousarray(np.asarray(L0_ffn_gamma, dtype=np.float32)).reshape(1, D)
    bet = np.ascontiguousarray(np.asarray(L0_ffn_beta, dtype=np.float32)).reshape(1, D)
    mask01 = np.triu(np.ones((128, 128), dtype=np.float32)).astype(ml_dtypes.bfloat16)
    idf = np.eye(128, dtype=np.float32)
    idb = np.eye(128, dtype=np.float32).astype(ml_dtypes.bfloat16)
    onesr = np.ones((1, 128), dtype=np.float32)

    if _NC_CACHE is None:
        _NC_CACHE = build()
    nc = _NC_CACHE
    in_maps = []
    for c in range(N_CORES):
        shard = np.ascontiguousarray(x[2 * c:2 * c + 2].reshape(TPC, D))
        in_maps.append(dict(x=shard, wq=wq, wkv=wkv, wo=wo, wf=wf,
                            gam=gam, bet=bet, mask01=mask01, idf=idf,
                            idb=idb, onesr=onesr))
    res = run_bass_kernel_spmd(nc, in_maps, core_ids=list(range(N_CORES)))
    out = np.concatenate(
        [np.asarray(res.results[c]["y"]).reshape(2, SEQ, D) for c in range(N_CORES)],
        axis=0)
    return out.astype(np.float32)
